# version 13
# baseline (speedup 1.0000x reference)
import numpy as np
import ml_dtypes
import concourse.bass as bass
import concourse.mybir as mybir
from concourse.bass_utils import run_bass_kernel_spmd
from contextlib import ExitStack

F32 = mybir.dt.float32
BF16 = mybir.dt.bfloat16
FP8 = mybir.dt.float8e4
I32 = mybir.dt.int32
AF = mybir.ActivationFunctionType
ALU = mybir.AluOpType

LN2C = float(np.log(2.0))
PI = float(np.pi)

S = 1024
SQ = 256
NQ = S // SQ
HT = 32
IC = 86
DM = 4096
DI = 11008


def build_program(debug=False, shared_q=True):
    nc = bass.Bass("TRN2", target_bir_lowering=False, debug=False)

    uT = nc.dram_tensor("uT", [DM, S], F32, kind="ExternalInput").ap()
    gwT = nc.dram_tensor("gwT", [DM, DI], F32, kind="ExternalInput").ap()
    upT = nc.dram_tensor("upT", [DM, DI], F32, kind="ExternalInput").ap()
    dwT = nc.dram_tensor("dwT", [DI, DM], F32, kind="ExternalInput").ap()
    lam = nc.dram_tensor("lam", [64, 2], F32, kind="ExternalInput").ap()
    pri = nc.dram_tensor("pri", [64, 2], F32, kind="ExternalInput").ap()
    qri = nc.dram_tensor("qri", [64, 2], F32, kind="ExternalInput").ap()
    vcr = nc.dram_tensor("vcr", [64, 64, 2], F32, kind="ExternalInput").ap()
    ctr = nc.dram_tensor("ctr", [1, 64, 2], F32, kind="ExternalInput").ap()
    bp = nc.dram_tensor("bp", [64, 1], F32, kind="ExternalInput").ap()
    sca = nc.dram_tensor("sca", [1, 8], F32, kind="ExternalInput").ap()

    outT = nc.dram_tensor("outT", [DM, S], F32, kind="ExternalOutput").ap()
    if debug == 2:
        dbgx1 = nc.dram_tensor("dbgx1", [DM, S], F32, kind="ExternalOutput").ap()
        dbghid = nc.dram_tensor("dbghid", [DI, S], FP8, kind="ExternalOutput").ap()
        dbgz2 = nc.dram_tensor("dbgz2", [DM, S], F32, kind="ExternalOutput").ap()
    if debug == 1:
        dbgK = nc.dram_tensor("dbgK", [1, 8192], F32, kind="ExternalOutput").ap()
        dbgz1 = nc.dram_tensor("dbgz1", [DM, SQ], F32, kind="ExternalOutput").ap()
        dbgx1 = nc.dram_tensor("dbgx1", [DM, S], F32, kind="ExternalOutput").ap()
        dbgqg = nc.dram_tensor("dbgqg", [128, HT * S], FP8, kind="ExternalOutput").ap()
        dbghid = nc.dram_tensor("dbghid", [DI, S], FP8, kind="ExternalOutput").ap()
        dbgz2 = nc.dram_tensor("dbgz2", [DM, S], F32, kind="ExternalOutput").ap()

    kmat = nc.dram_tensor("kmat", [128, 8320], BF16, kind="Internal").ap()
    x1sp = nc.dram_tensor("x1sp", [DM, S], F32, kind="Internal").ap()
    hidsp = nc.dram_tensor("hidsp", [DI, S], FP8, kind="Internal").ap()

    with ExitStack() as ctx:
        sb = lambda nm, shp, dt: ctx.enter_context(nc.sbuf_tensor(nm, shp, dt))
        arena = sb("arena", [128, 2 * HT * S], FP8)
        qg = arena.ap()[:, 0:HT * S]
        qu = arena.ap()[:, HT * S:2 * HT * S]
        z2b = arena.ap().bitcast(BF16)
        utq = sb("utq", [128, 16384], BF16)
        z1b = sb("z1b", [128, 16384], BF16)
        NSLOT = 6
        dbr = utq.ap()[:, 0:512 * NSLOT]
        hbr = utq.ap().bitcast(FP8)[:, 8192:8192 + 2048 * NSLOT]
        hob = sb("hob", [128, 2048], FP8)
        t0 = sb("t0", [128, 1024], F32)
        t1 = sb("t1", [128, 1024], F32)
        t2 = sb("t2", [128, 1024], F32)
        t3 = sb("t3", [128, 1024], F32)
        t4 = sb("t4", [128, 1024], F32)
        t5 = sb("t5", [128, 1024], F32)
        ti = sb("ti", [128, 1024], I32)
        zb8 = sb("zb8", [128, 2048], BF16)
        toepbig = sb("toepbig", [128, 4096], BF16)
        ursd = sb("ursd", [128, 512], F32)
        ksm = sb("ksm", [64, 2048], F32)
        kbb = sb("kbb", [64, 1024], BF16)
        vcs = sb("vcs", [64, 128], F32)
        vcb = sb("vcb", [64, 128], BF16)
        smc = sb("smc", [64, 16], F32)
        v8b = sb("v8b", [64, 8], BF16)
        colp = sb("colp", [128, 16], F32)
        rowf = sb("rowf", [1, 2048], F32)
        rowb = sb("rowb", [1, 4096], BF16)
        onesc = sb("onesc", [128, 1], BF16)
        ones1 = sb("ones1", [1, 128], BF16)
        idn = sb("idn", [128, 128], BF16)
        biasp = sb("biasp", [128, 8], F32)
        iot = sb("iot", [128, 64], I32)
        mrow = sb("mrow", [1, 64], BF16)
        scs = sb("scs", [1, 36], F32)

        ps = [ctx.enter_context(nc.psum_tensor(f"ps{k}", [128, 512], F32)) for k in range(8)]

        sem = lambda nm: ctx.enter_context(nc.semaphore(nm))
        sems = {nm: sem(nm) for nm in ("sQ", "aQ", "gQ", "tM", "vM", "sM", "gM", "oQ",
                               "tQ", "uQ0", "uQ1", "aQ0", "aQ1", "hsQ0", "hsQ1",
                               "gQ0", "gQ1", "gQb", "oQb", "hQ0", "hQ1", "hQ2", "hQ3",
                               "dQ0", "dQ1", "dQ2", "dQ3", "dQ4", "dQ5", "hQ4", "hQ5",
                               "xQ0", "xQ1", "xQ2", "xQ3")}

        block = ctx.enter_context(nc.Block())

        ops = {e: [] for e in ("sync", "scalar", "vector", "gpsimd", "tensor")}
        cnt = {k: 0 for k in sems}

        def emit(eng, fn):
            ops[eng].append(fn)

        def dma(eng, cname, out, in_, slow=False):
            cnt[cname] += 16
            s_ = sems[cname]
            emit(eng, lambda e, o=out, i=in_, s2=s_, sl=slow: e.dma_start(out=o, in_=i, allow_slow_non_contiguous=sl).then_inc(s2, 16))
            return cnt[cname]

        def wait(eng, cname, val=None):
            v = cnt[cname] if val is None else val
            s_ = sems[cname]
            emit(eng, lambda e, s2=s_, v2=v: e.wait_ge(s2, v2))

        last = {e: None for e in ops}

        def mark(eng, cname):
            cnt[cname] += 1
            s_ = sems[cname]
            def run(e, s2=s_, eng2=eng):
                li = last[eng2]
                if li is not None:
                    li.then_inc(s2, 1)
                    last[eng2] = None
                else:
                    e.drain()
                    e.sem_inc(s2, 1)
            emit(eng, run)
            return cnt[cname]

        def mark_hard(eng, cname):
            cnt[cname] += 1
            s_ = sems[cname]
            def run(e, s2=s_, eng2=eng):
                last[eng2] = None
                e.drain()
                e.sem_inc(s2, 1)
            emit(eng, run)
            return cnt[cname]

        def op(eng, fn):
            def run(e, f=fn, eng2=eng):
                last[eng2] = f(e)
                e.drain()
            emit(eng, run)

        def opn(eng, fn):
            def run(e, f=fn, eng2=eng):
                last[eng2] = f(e)
            emit(eng, run)

        KS = lambda k2: ksm[:, 64 * k2:64 * (k2 + 1)]

        dma("sync", "sQ", smc[:, 0:1], lam[:, 0:1], slow=True)
        dma("sync", "sQ", smc[:, 1:2], lam[:, 1:2], slow=True)
        dma("sync", "sQ", smc[:, 2:3], pri[:, 0:1], slow=True)
        dma("sync", "sQ", smc[:, 3:4], pri[:, 1:2], slow=True)
        dma("sync", "sQ", smc[:, 4:5], qri[:, 0:1], slow=True)
        dma("sync", "sQ", smc[:, 5:6], qri[:, 1:2], slow=True)
        dma("sync", "sQ", smc[:, 6:7], ctr[0, :, 0:1], slow=True)
        dma("sync", "sQ", smc[:, 7:8], ctr[0, :, 1:2], slow=True)
        dma("sync", "sQ", smc[:, 8:9], bp)
        dma("sync", "sQ", vcs[:, 0:64], vcr[:, :, 0], slow=True)
        dma("sync", "sQ", vcs[:, 64:128], vcr[:, :, 1], slow=True)
        dma("sync", "sQ", scs[0:1, 0:8], sca)
        small_loads = cnt["sQ"]

        op("vector", lambda v: v.memset(onesc[:], 1.0))
        op("vector", lambda v: v.memset(ones1[:], 1.0))
        op("vector", lambda v: v.memset(biasp[:, 0:1], -3.0 * LN2C))
        op("vector", lambda v: v.memset(biasp[:, 1:2], 3.0 * LN2C))
        op("vector", lambda v: v.memset(biasp[:, 2:3], PI / 2.0))
        op("vector", lambda v: v.memset(biasp[:, 3:4], -4.0 * LN2C))
        op("vector", lambda v: v.memset(biasp[:, 4:5], 1e-38))

        op("gpsimd", lambda g: g.iota(iot[:, 0:64], pattern=[[-1, 64]], base=0, channel_multiplier=1))
        gm_i1 = mark("gpsimd", "gM")
        wait("vector", "gM", gm_i1)
        op("vector", lambda v: v.tensor_scalar(out=idn[:, 0:64], in0=iot[:, 0:64], scalar1=0, scalar2=None, op0=ALU.is_equal))
        vm_iot1 = mark("vector", "vM")
        wait("gpsimd", "vM", vm_iot1)
        op("gpsimd", lambda g: g.iota(iot[:, 0:64], pattern=[[-1, 64]], base=-64, channel_multiplier=1))
        gm_i2 = mark("gpsimd", "gM")
        wait("vector", "gM", gm_i2)
        op("vector", lambda v: v.tensor_scalar(out=idn[:, 64:128], in0=iot[:, 0:64], scalar1=0, scalar2=None, op0=ALU.is_equal))
        mark("vector", "vM")
        vm_idn = cnt["vM"]

        wait("scalar", "sQ", small_loads)
        op("scalar", lambda s_: s_.activation(out=scs[0:1, 16:17], in_=scs[0:1, 1:2], func=AF.Exp))
        sm_step = mark("scalar", "sM")
        wait("vector", "sM", sm_step)
        wait("vector", "sQ", small_loads)
        op("vector", lambda v: v.tensor_scalar(out=scs[0:1, 8:9], in0=scs[0:1, 0:1], scalar1=1.0, scalar2=None, op0=ALU.add))
        op("vector", lambda v: v.reciprocal(out=scs[0:1, 17:18], in_=scs[0:1, 16:17]))
        op("vector", lambda v: v.tensor_scalar(out=scs[0:1, 9:10], in0=scs[0:1, 17:18], scalar1=2.0, scalar2=None, op0=ALU.mult))
        op("vector", lambda v: v.reciprocal(out=scs[0:1, 10:11], in_=scs[0:1, 5:6]))
        op("vector", lambda v: v.reciprocal(out=scs[0:1, 11:12], in_=scs[0:1, 6:7]))
        op("vector", lambda v: v.reciprocal(out=scs[0:1, 12:13], in_=scs[0:1, 7:8]))
        op("vector", lambda v: v.tensor_tensor(out=scs[0:1, 13:14], in0=scs[0:1, 2:3], in1=scs[0:1, 5:6], op=ALU.mult))
        op("vector", lambda v: v.tensor_tensor(out=scs[0:1, 14:15], in0=scs[0:1, 3:4], in1=scs[0:1, 6:7], op=ALU.mult))
        op("vector", lambda v: v.tensor_tensor(out=scs[0:1, 18:19], in0=scs[0:1, 4:5], in1=scs[0:1, 7:8], op=ALU.mult))
        op("vector", lambda v: v.tensor_scalar(out=scs[0:1, 15:16], in0=scs[0:1, 18:19], scalar1=2.0, scalar2=None, op0=ALU.mult))
        op("vector", lambda v: v.tensor_copy(out=rowb[0:1, 0:16], in_=scs[0:1, 0:16]))
        op("vector", lambda v: v.tensor_copy(out=scs[0:1, 20:36], in_=rowb[0:1, 0:16]))
        op("vector", lambda v: v.tensor_tensor(out=scs[0:1, 20:36], in0=scs[0:1, 0:16], in1=scs[0:1, 20:36], op=ALU.subtract))
        op("vector", lambda v: v.tensor_copy(out=rowb[0:1, 16:32], in_=scs[0:1, 20:36]))
        vm_scal = mark("vector", "vM")

        wait("tensor", "vM", vm_scal)
        for dst, src in [(0, 8), (1, 10), (2, 11), (3, 12), (4, 13), (5, 14), (6, 15)]:
            opn("tensor", lambda t, s2=src: t.matmul(ps[0][:, 0:1], ones1[0:1, :], rowb[0:1, s2:s2 + 1], start=True, stop=False))
            opn("tensor", lambda t, s2=src: t.matmul(ps[0][:, 0:1], ones1[0:1, :], rowb[0:1, 16 + s2:17 + s2], start=False, stop=True))
            tm = mark("tensor", "tM")
            wait("vector", "tM", tm)
            op("vector", lambda v, d=dst: v.tensor_copy(out=colp[:, d:d + 1], in_=ps[0][:, 0:1]))
            vm = mark("vector", "vM")
            wait("tensor", "vM", vm)
        opn("tensor", lambda t: t.matmul(ps[1][0:64, 0:1], ones1[0:1, 0:64], rowb[0:1, 9:10], start=True, stop=False))
        opn("tensor", lambda t: t.matmul(ps[1][0:64, 0:1], ones1[0:1, 0:64], rowb[0:1, 25:26], start=False, stop=True))
        tm = mark("tensor", "tM")
        wait("vector", "tM", tm)
        op("vector", lambda v: v.tensor_copy(out=smc[:, 10:11], in_=ps[1][0:64, 0:1]))
        op("vector", lambda v: v.tensor_scalar(out=smc[:, 9:10], in0=smc[:, 0:1], scalar1=-1.0, scalar2=None, op0=ALU.mult))
        vm_cols = mark("vector", "vM")

        op("vector", lambda v: v.tensor_copy(out=vcb[:, 0:64], in_=vcs[:, 0:64]))
        op("vector", lambda v: v.tensor_copy(out=vcb[:, 64:128], in_=vcs[:, 64:128]))
        op("vector", lambda v: v.tensor_copy(out=kbb[:, 0:1], in_=smc[:, 8:9]))
        vm = mark("vector", "vM")
        wait("tensor", "vM", vm)
        wait("tensor", "vM", vm_idn)
        opn("tensor", lambda t: t.transpose(ps[2][0:64, 0:32].bitcast(BF16), vcb[:, 0:64], idn[0:64, 0:64]))
        opn("tensor", lambda t: t.transpose(ps[3][0:64, 0:32].bitcast(BF16), vcb[:, 64:128], idn[0:64, 0:64]))
        tm = mark("tensor", "tM")
        wait("vector", "tM", tm)
        op("vector", lambda v: v.tensor_copy(out=vcb[:, 0:64], in_=ps[2][0:64, 0:32].bitcast(BF16)))
        op("vector", lambda v: v.tensor_copy(out=vcb[:, 64:128], in_=ps[3][0:64, 0:32].bitcast(BF16)))
        vm = mark("vector", "vM")
        wait("tensor", "vM", vm)
        opn("tensor", lambda t: t.matmul(ps[2][0:64, 0:1], vcb[:, 0:64], kbb[:, 0:1], start=True, stop=True))
        opn("tensor", lambda t: t.matmul(ps[3][0:64, 0:1], vcb[:, 64:128], kbb[:, 0:1], start=True, stop=True))
        tm = mark("tensor", "tM")
        wait("vector", "tM", tm)
        op("vector", lambda v: v.tensor_copy(out=smc[:, 11:12], in_=ps[2][0:64, 0:1]))
        op("vector", lambda v: v.tensor_copy(out=smc[:, 12:13], in_=ps[3][0:64, 0:1]))

        def cmul_conj(v, are, aim, bre, bim, outre, outim):
            v.tensor_tensor(out=t5[0:64, 0:1], in0=smc[:, are:are + 1], in1=smc[:, bre:bre + 1], op=ALU.mult); v.drain()
            v.tensor_tensor(out=t5[0:64, 1:2], in0=smc[:, aim:aim + 1], in1=smc[:, bim:bim + 1], op=ALU.mult); v.drain()
            v.tensor_tensor(out=t5[0:64, 4:5], in0=t5[0:64, 0:1], in1=t5[0:64, 1:2], op=ALU.add); v.drain()
            v.tensor_tensor(out=t5[0:64, 2:3], in0=smc[:, are:are + 1], in1=smc[:, bim:bim + 1], op=ALU.mult); v.drain()
            v.tensor_tensor(out=t5[0:64, 3:4], in0=smc[:, aim:aim + 1], in1=smc[:, bre:bre + 1], op=ALU.mult); v.drain()
            v.tensor_tensor(out=t5[0:64, 5:6], in0=t5[0:64, 2:3], in1=t5[0:64, 3:4], op=ALU.subtract); v.drain()
            v.tensor_copy(out=v8b[:, outre:outre + 1], in_=t5[0:64, 4:5]); v.drain()
            v.tensor_copy(out=v8b[:, outim:outim + 1], in_=t5[0:64, 5:6])
        op("vector", lambda v: cmul_conj(v, 6, 7, 11, 12, 0, 1))
        op("vector", lambda v: cmul_conj(v, 6, 7, 2, 3, 2, 3))
        op("vector", lambda v: cmul_conj(v, 4, 5, 11, 12, 4, 5))
        op("vector", lambda v: cmul_conj(v, 4, 5, 2, 3, 6, 7))
        vm_v8 = mark("vector", "vM")

        zco = t0[0:64, 0:512]
        zsi = t0[0:64, 512:1024]
        zdre = t1[0:64, 0:512]
        zdim = t1[0:64, 512:1024]
        zt = t2[0:64, 0:512]
        zt2 = t2[0:64, 512:1024]
        zrec = t3[0:64, 0:512]
        zji = t3[0:64, 512:1024]
        crebf = zb8[0:64, 0:512]
        cimbf = zb8[0:64, 512:1024]
        for cc4 in range(8):
            op("gpsimd", lambda g, c=cc4: g.iota(zji, pattern=[[1, 512]], base=512 * c, channel_multiplier=0, allow_small_or_imprecise_dtypes=True))
            gm = mark("gpsimd", "gM")
            wait("scalar", "gM", gm)
            if cc4 == 0:
                wait("scalar", "vM", vm_cols)
            op("scalar", lambda s_: s_.activation(out=zco, in_=zji, func=AF.Sin, scale=-PI / 4096.0, bias=biasp[0:64, 2:3]))
            op("scalar", lambda s_: s_.activation(out=zsi, in_=zji, func=AF.Sin, scale=PI / 4096.0))
            sm = mark("scalar", "sM")
            wait("gpsimd", "sM", sm)
            wait("vector", "sM", sm)
            op("vector", lambda v: v.tensor_scalar(out=zdre, in0=zco, scalar1=smc[:, 9:10], scalar2=None, op0=ALU.mult))
            op("vector", lambda v: v.tensor_scalar(out=zt, in0=zsi, scalar1=smc[:, 10:11], scalar2=None, op0=ALU.mult))
            op("vector", lambda v: v.tensor_scalar(out=zt2, in0=zco, scalar1=smc[:, 1:2], scalar2=None, op0=ALU.mult))
            op("vector", lambda v: v.tensor_tensor(out=zdim, in0=zt, in1=zt2, op=ALU.subtract))
            vm = mark("vector", "vM")
            wait("scalar", "vM", vm)
            op("scalar", lambda s_: s_.activation(out=zt, in_=zdre, func=AF.Square))
            op("scalar", lambda s_: s_.activation(out=zt2, in_=zdim, func=AF.Square))
            sm = mark("scalar", "sM")
            wait("vector", "sM", sm)
            op("vector", lambda v: v.tensor_tensor(out=zrec, in0=zt, in1=zt2, op=ALU.add))
            op("vector", lambda v: v.reciprocal(out=zrec, in_=zrec))
            op("vector", lambda v: v.tensor_tensor(out=crebf, in0=zdre, in1=zrec, op=ALU.mult))
            op("vector", lambda v: v.tensor_scalar(out=zt, in0=zdim, scalar1=-1.0, scalar2=None, op0=ALU.mult))
            op("vector", lambda v: v.tensor_tensor(out=cimbf, in0=zt, in1=zrec, op=ALU.mult))
            vm = mark("vector", "vM")
            wait("tensor", "vM", vm)
            if cc4 == 0:
                wait("tensor", "vM", vm_v8)
            for sc16 in range(8):
                q64 = 8 * cc4 + sc16
                opn("tensor", lambda t, s2=sc16, q2=q64: t.matmul(ps[4][0:64, 8 * q2:8 * q2 + 8], crebf[:, 64 * s2:64 * s2 + 64], v8b[:, 0:8], start=True, stop=True))
                opn("tensor", lambda t, s2=sc16, q2=q64: t.matmul(ps[5][0:64, 8 * q2:8 * q2 + 8], cimbf[:, 64 * s2:64 * s2 + 64], v8b[:, 0:8], start=True, stop=True))
            tm = mark("tensor", "tM")
            wait("vector", "tM", tm)
        op("vector", lambda v: v.tensor_copy(out=t4[0:64, 0:512], in_=ps[4][0:64, 0:512]))
        op("vector", lambda v: v.tensor_copy(out=t4[0:64, 512:1024], in_=ps[5][0:64, 0:512]))
        mA = t4[0:64, 0:512]
        mB = t4[0:64, 512:1024]
        for ra, rb, sign, dst in [(0, 1, -1, 0), (1, 0, +1, 1), (2, 3, -1, 2), (3, 2, +1, 3),
                                  (4, 5, -1, 4), (5, 4, +1, 5), (6, 7, -1, 6), (7, 6, +1, 7)]:
            aop = ALU.subtract if sign < 0 else ALU.add
            op("vector", lambda v, a=ra, b=rb, d=dst, o2=aop: v.tensor_tensor(
                out=KS(d), in0=mA[:, a::8], in1=mB[:, b::8], op=o2))
        op("gpsimd", lambda g: g.iota(iot[0:64, 0:64], pattern=[[64, 64]], base=0, channel_multiplier=1))
        gm = mark("gpsimd", "gM")
        wait("scalar", "gM", gm)
        op("scalar", lambda s_: s_.activation(out=KS(8), in_=iot[0:64, 0:64], func=AF.Sin, scale=-PI / 4096.0, bias=biasp[0:64, 2:3]))
        op("scalar", lambda s_: s_.activation(out=KS(9), in_=iot[0:64, 0:64], func=AF.Sin, scale=PI / 4096.0))
        sm = mark("scalar", "sM")
        wait("vector", "sM", sm)
        wait("gpsimd", "sM", sm)

        def ar_compute(v):
            co2, s2v = KS(8), KS(9)
            v.tensor_tensor(out=KS(10), in0=co2, in1=KS(6), op=ALU.mult); v.drain()
            v.tensor_scalar(out=KS(10), in0=KS(10), scalar1=1.0, scalar2=None, op0=ALU.add); v.drain()
            v.tensor_tensor(out=KS(11), in0=co2, in1=KS(7), op=ALU.mult); v.drain()
            v.tensor_tensor(out=KS(12), in0=KS(2), in1=KS(4), op=ALU.mult); v.drain()
            v.tensor_tensor(out=KS(14), in0=KS(3), in1=KS(5), op=ALU.mult); v.drain()
            v.tensor_tensor(out=KS(12), in0=KS(12), in1=KS(14), op=ALU.subtract); v.drain()
            v.tensor_tensor(out=KS(13), in0=KS(2), in1=KS(5), op=ALU.mult); v.drain()
            v.tensor_tensor(out=KS(14), in0=KS(3), in1=KS(4), op=ALU.mult); v.drain()
            v.tensor_tensor(out=KS(13), in0=KS(13), in1=KS(14), op=ALU.add); v.drain()
            v.tensor_tensor(out=KS(14), in0=KS(10), in1=KS(10), op=ALU.mult); v.drain()
            v.tensor_tensor(out=KS(15), in0=KS(11), in1=KS(11), op=ALU.mult); v.drain()
            v.tensor_tensor(out=KS(14), in0=KS(14), in1=KS(15), op=ALU.add); v.drain()
            v.reciprocal(out=KS(14), in_=KS(14)); v.drain()
            v.tensor_tensor(out=KS(15), in0=KS(12), in1=KS(10), op=ALU.mult); v.drain()
            v.tensor_tensor(out=t0[0:64, 0:64], in0=KS(13), in1=KS(11), op=ALU.mult); v.drain()
            v.tensor_tensor(out=KS(15), in0=KS(15), in1=t0[0:64, 0:64], op=ALU.add); v.drain()
            v.tensor_tensor(out=t0[0:64, 0:64], in0=KS(13), in1=KS(10), op=ALU.mult); v.drain()
            v.tensor_tensor(out=t0[0:64, 64:128], in0=KS(12), in1=KS(11), op=ALU.mult); v.drain()
            v.tensor_tensor(out=KS(12), in0=t0[0:64, 0:64], in1=t0[0:64, 64:128], op=ALU.subtract); v.drain()
            v.tensor_tensor(out=KS(15), in0=KS(15), in1=KS(14), op=ALU.mult); v.drain()
            v.tensor_tensor(out=KS(12), in0=KS(12), in1=KS(14), op=ALU.mult); v.drain()
            v.tensor_tensor(out=KS(10), in0=co2, in1=KS(15), op=ALU.mult); v.drain()
            v.tensor_tensor(out=KS(10), in0=KS(0), in1=KS(10), op=ALU.subtract); v.drain()
            v.tensor_tensor(out=KS(11), in0=co2, in1=KS(12), op=ALU.mult); v.drain()
            v.tensor_tensor(out=KS(11), in0=KS(1), in1=KS(11), op=ALU.subtract); v.drain()
            v.tensor_tensor(out=KS(13), in0=co2, in1=KS(10), op=ALU.mult); v.drain()
            v.tensor_tensor(out=KS(15), in0=s2v, in1=KS(11), op=ALU.mult); v.drain()
            v.tensor_tensor(out=KS(13), in0=KS(13), in1=KS(15), op=ALU.subtract); v.drain()
            v.tensor_tensor(out=KS(14), in0=co2, in1=KS(11), op=ALU.mult); v.drain()
            v.tensor_tensor(out=KS(15), in0=s2v, in1=KS(10), op=ALU.mult); v.drain()
            v.tensor_tensor(out=KS(14), in0=KS(14), in1=KS(15), op=ALU.add); v.drain()
            v.tensor_copy(out=kbb[:, 64:128], in_=KS(13)); v.drain()
            v.tensor_copy(out=kbb[:, 128:192], in_=KS(14))
        op("vector", ar_compute)
        vm = mark("vector", "vM")
        wait("tensor", "vM", vm)
        opn("tensor", lambda t: t.transpose(ps[6][0:64, 0:32].bitcast(BF16), kbb[:, 64:128], idn[0:64, 0:64]))
        opn("tensor", lambda t: t.transpose(ps[7][0:64, 0:32].bitcast(BF16), kbb[:, 128:192], idn[0:64, 0:64]))
        tm = mark("tensor", "tM")
        wait("vector", "tM", tm)
        op("vector", lambda v: v.tensor_copy(out=kbb[:, 192:256], in_=ps[6][0:64, 0:32].bitcast(BF16)))
        op("vector", lambda v: v.tensor_copy(out=kbb[:, 256:320], in_=ps[7][0:64, 0:32].bitcast(BF16)))
        op("vector", lambda v: v.tensor_scalar(out=t0[0:64, 0:64], in0=ps[7][0:64, 0:32].bitcast(BF16), scalar1=-1.0, scalar2=None, op0=ALU.mult))
        op("vector", lambda v: v.tensor_copy(out=kbb[:, 320:384], in_=t0[0:64, 0:64]))
        vm_ar = mark("vector", "vM")

        op("gpsimd", lambda g: g.iota(iot[0:1, 0:64], pattern=[[1, 64]], base=0, channel_multiplier=0))
        gm = mark("gpsimd", "gM")
        wait("vector", "gM", gm)
        op("vector", lambda v: v.tensor_copy(out=mrow[0:1, 0:64], in_=iot[0:1, 0:64]))
        vm = mark("vector", "vM")
        wait("tensor", "vM", vm)
        opn("tensor", lambda t: t.matmul(ps[6][0:64, 64:128], mrow[0:1, 0:64], mrow[0:1, 0:64], start=True, stop=True))
        tm = mark("tensor", "tM")
        wait("vector", "tM", tm)
        op("vector", lambda v: v.tensor_copy(out=KS(10), in_=ps[6][0:64, 64:128]))

        def dft_red(v):
            prod = KS(10)
            v.tensor_scalar(out=t0[0:64, 0:64], in0=prod, scalar1=1.0 / 64.0, scalar2=1.0 / 128.0, op0=ALU.mult, op1=ALU.add); v.drain()
            v.tensor_copy(out=ti[0:64, 0:64], in_=t0[0:64, 0:64]); v.drain()
            v.tensor_scalar(out=t1[0:64, 0:64], in0=prod, scalar1=1.0 / 64.0, scalar2=None, op0=ALU.mult); v.drain()
            v.tensor_tensor(out=KS(11), in0=t1[0:64, 0:64], in1=ti[0:64, 0:64], op=ALU.subtract); v.drain()
            v.tensor_scalar(out=ti[0:64, 0:64], in0=KS(11), scalar1=0.25, scalar2=None, op0=ALU.add); v.drain()
            v.tensor_tensor(out=KS(11), in0=KS(11), in1=ti[0:64, 0:64], op=ALU.subtract); v.drain()
            v.tensor_scalar(out=t0[0:64, 0:64], in0=prod, scalar1=1.0 / 4096.0, scalar2=1.0 / 8192.0, op0=ALU.mult, op1=ALU.add); v.drain()
            v.tensor_copy(out=ti[0:64, 0:64], in_=t0[0:64, 0:64]); v.drain()
            v.tensor_scalar(out=t1[0:64, 0:64], in0=prod, scalar1=1.0 / 4096.0, scalar2=None, op0=ALU.mult); v.drain()
            v.tensor_tensor(out=KS(12), in0=t1[0:64, 0:64], in1=ti[0:64, 0:64], op=ALU.subtract); v.drain()
            v.tensor_scalar(out=ti[0:64, 0:64], in0=KS(12), scalar1=0.25, scalar2=None, op0=ALU.add); v.drain()
            v.tensor_tensor(out=KS(12), in0=KS(12), in1=ti[0:64, 0:64], op=ALU.subtract)
        op("vector", dft_red)
        vm = mark("vector", "vM")
        wait("scalar", "vM", vm)
        op("scalar", lambda s_: s_.activation(out=KS(13), in_=KS(11), func=AF.Abs))
        op("scalar", lambda s_: s_.activation(out=KS(14), in_=KS(12), func=AF.Abs))
        op("scalar", lambda s_: s_.activation(out=t0[0:64, 0:64], in_=KS(13), func=AF.Sin, scale=-PI, bias=biasp[0:64, 2:3]))
        op("scalar", lambda s_: s_.activation(out=t0[0:64, 64:128], in_=KS(11), func=AF.Sin, scale=PI))
        op("scalar", lambda s_: s_.activation(out=t0[0:64, 128:192], in_=KS(14), func=AF.Sin, scale=-PI, bias=biasp[0:64, 2:3]))
        op("scalar", lambda s_: s_.activation(out=t0[0:64, 192:256], in_=KS(12), func=AF.Sin, scale=PI))
        op("scalar", lambda s_: s_.activation(out=t1[0:64, 0:64], in_=t0[0:64, 64:128], func=AF.Square))
        op("scalar", lambda s_: s_.activation(out=t1[0:64, 64:128], in_=t0[0:64, 192:256], func=AF.Square))
        sm = mark("scalar", "sM")
        wait("vector", "sM", sm)
        op("vector", lambda v: v.tensor_scalar(out=kbb[:, 64 * 6:64 * 7], in0=t1[0:64, 0:64], scalar1=-2.0, scalar2=1.0, op0=ALU.mult, op1=ALU.add))
        op("vector", lambda v: v.scalar_tensor_tensor(out=kbb[:, 64 * 7:64 * 8], in0=t0[0:64, 64:128], scalar=2.0, in1=t0[0:64, 0:64], op0=ALU.mult, op1=ALU.mult))
        op("vector", lambda v: v.tensor_scalar(out=kbb[:, 64 * 8:64 * 9], in0=t1[0:64, 64:128], scalar1=-2.0, scalar2=1.0, op0=ALU.mult, op1=ALU.add))
        op("vector", lambda v: v.scalar_tensor_tensor(out=kbb[:, 64 * 9:64 * 10], in0=t0[0:64, 192:256], scalar=2.0, in1=t0[0:64, 128:192], op0=ALU.mult, op1=ALU.mult))
        vm_dft = mark("vector", "vM")

        W64re = kbb[:, 64 * 6:64 * 7]
        W64im = kbb[:, 64 * 7:64 * 8]
        T4re = kbb[:, 64 * 8:64 * 9]
        T4im = kbb[:, 64 * 9:64 * 10]
        ARre = kbb[:, 192:256]
        ARim = kbb[:, 256:320]
        ARimN = kbb[:, 320:384]
        wait("tensor", "vM", vm_dft)
        opn("tensor", lambda t: t.matmul(ps[6][0:64, 128:192], ARre, W64re, start=True, stop=False))
        opn("tensor", lambda t: t.matmul(ps[6][0:64, 128:192], ARimN, W64im, start=False, stop=True))
        opn("tensor", lambda t: t.matmul(ps[7][0:64, 128:192], ARre, W64im, start=True, stop=False))
        opn("tensor", lambda t: t.matmul(ps[7][0:64, 128:192], ARim, W64re, start=False, stop=True))
        tm = mark("tensor", "tM")
        wait("vector", "tM", tm)

        def twiddle(v):
            Are = t0[0:64, 0:64]; Aim = t0[0:64, 64:128]
            v.tensor_copy(out=Are, in_=ps[6][0:64, 128:192]); v.drain()
            v.tensor_copy(out=Aim, in_=ps[7][0:64, 128:192]); v.drain()
            v.tensor_copy(out=t1[0:64, 0:64], in_=T4re); v.drain()
            v.tensor_copy(out=t1[0:64, 64:128], in_=T4im); v.drain()
            v.tensor_tensor(out=t2[0:64, 0:64], in0=Are, in1=t1[0:64, 0:64], op=ALU.mult); v.drain()
            v.tensor_tensor(out=t2[0:64, 64:128], in0=Aim, in1=t1[0:64, 64:128], op=ALU.mult); v.drain()
            v.tensor_tensor(out=t2[0:64, 0:64], in0=t2[0:64, 0:64], in1=t2[0:64, 64:128], op=ALU.subtract); v.drain()
            v.tensor_tensor(out=t2[0:64, 128:192], in0=Are, in1=t1[0:64, 64:128], op=ALU.mult); v.drain()
            v.tensor_tensor(out=t2[0:64, 192:256], in0=Aim, in1=t1[0:64, 0:64], op=ALU.mult); v.drain()
            v.tensor_tensor(out=t2[0:64, 64:128], in0=t2[0:64, 128:192], in1=t2[0:64, 192:256], op=ALU.add); v.drain()
            v.tensor_copy(out=kbb[:, 192:256], in_=t2[0:64, 0:64]); v.drain()
            v.tensor_copy(out=kbb[:, 256:320], in_=t2[0:64, 64:128]); v.drain()
            v.tensor_scalar(out=t2[0:64, 64:128], in0=t2[0:64, 64:128], scalar1=-1.0, scalar2=None, op0=ALU.mult); v.drain()
            v.tensor_copy(out=kbb[:, 320:384], in_=t2[0:64, 64:128])
        op("vector", twiddle)
        vm = mark("vector", "vM")
        wait("tensor", "vM", vm)
        opn("tensor", lambda t: t.matmul(ps[6][0:64, 192:256], W64re, kbb[:, 192:256], start=True, stop=False))
        opn("tensor", lambda t: t.matmul(ps[6][0:64, 192:256], W64im, kbb[:, 320:384], start=False, stop=True))
        tm = mark("tensor", "tM")
        wait("vector", "tM", tm)
        op("vector", lambda v: v.tensor_scalar(out=t3[0:64, 0:64], in0=ps[6][0:64, 192:256], scalar1=1.0 / 4096.0, scalar2=None, op0=ALU.mult))
        op("vector", lambda v: v.tensor_copy(out=kbb[:, 0:64], in_=t3[0:64, 0:64]))
        op("vector", lambda v: v.memset(z1b[0:1, 0:4096], 0.0))
        vm_k = mark("vector", "vM")
        wait("sync", "vM", vm_k)
        krow_dst = z1b[0:1, 4096:8192].rearrange("o (n2 n1) -> o n2 n1", n1=64)
        dma("sync", "sQ", krow_dst, kbb[:, 0:64])
        wait("sync", "sQ")
        for p in range(128):
            dma("sync", "sQ", kmat[p:p + 1, p:p + 8192], z1b[0:1, 0:8192])
        kmat_done = cnt["sQ"]
        if debug == 1:
            dma("gpsimd", "gQ", dbgK, z1b[0:1, 0:8192])

        def quantize(src_f32, invcol, qbias_col, clip, dst8, W=1024):
            if clip:
                op("vector", lambda v: v.tensor_scalar(out=t4[:, 0:W], in0=src_f32, scalar1=colp[:, invcol:invcol + 1], scalar2=448.0, op0=ALU.mult, op1=ALU.min))
                op("vector", lambda v: v.tensor_scalar(out=t4[:, 0:W], in0=t4[:, 0:W], scalar1=-448.0, scalar2=None, op0=ALU.max))
            else:
                op("vector", lambda v: v.tensor_scalar(out=t4[:, 0:W], in0=src_f32, scalar1=colp[:, invcol:invcol + 1], scalar2=None, op0=ALU.mult))
            vm = mark("vector", "vM")
            wait("scalar", "vM", vm)
            op("scalar", lambda s_: s_.activation(out=t5[:, 0:W], in_=t4[:, 0:W], func=AF.Square))
            op("scalar", lambda s_: s_.activation(out=t5[:, 0:W], in_=t5[:, 0:W], func=AF.Ln, bias=biasp[:, 4:5]))
            sm = mark("scalar", "sM")
            wait("vector", "sM", sm)
            op("vector", lambda v: v.tensor_scalar(out=ti[:, 0:W], in0=t5[:, 0:W], scalar1=0.5 / LN2C, scalar2=-0.5, op0=ALU.mult, op1=ALU.add))
            op("vector", lambda v: v.tensor_scalar(out=ti[:, 0:W], in0=ti[:, 0:W], scalar1=-6, scalar2=8, op0=ALU.max, op1=ALU.min))
            vm = mark("vector", "vM")
            wait("scalar", "vM", vm)
            op("scalar", lambda s_, qb=qbias_col: s_.activation(out=t5[:, 0:W], in_=ti[:, 0:W], func=AF.Exp, scale=LN2C, bias=biasp[:, qb:qb + 1]))
            op("scalar", lambda s_: s_.activation(out=t0[:, 0:W], in_=ti[:, 0:W], func=AF.Exp, scale=-LN2C, bias=biasp[:, 1:2]))
            sm = mark("scalar", "sM")
            wait("vector", "sM", sm)
            op("vector", lambda v: v.tensor_tensor(out=t4[:, 0:W], in0=t4[:, 0:W], in1=t0[:, 0:W], op=ALU.mult))
            op("vector", lambda v: v.tensor_copy(out=ti[:, 0:W], in_=t4[:, 0:W]))
            op("vector", lambda v, d=dst8: v.tensor_tensor(out=d, in0=ti[:, 0:W], in1=t5[:, 0:W], op=ALU.mult))
            return mark("vector", "vM")

        UQ = lambda P, jt: utq[:, 8192 * P + 256 * jt:8192 * P + 256 * (jt + 1)]
        Z1 = lambda P, it: z1b[:, 8192 * P + 256 * it:8192 * P + 256 * (it + 1)]
        wait("sync", "sQ", kmat_done)
        tb_ld = dma("sync", "tQ", toepbig[:, 0:4096], kmat[:, 4096:8192])

        usrc = uT.rearrange("(jt p) s -> p jt s", p=128)
        x1spr = x1sp.rearrange("(t p) s -> p t s", p=128)
        qgr = qg.rearrange("p (t s) -> p t s", s=S)
        qur = qu.rearrange("p (t s) -> p t s", s=S)
        evac_done = {}
        sq_mark = {}
        mm_mark = {}
        conv_end = {}
        params_ps_read = {}

        def emit_stats(qq, it):
            P = qq % 2
            wait("tensor", "vM", evac_done[(qq, it)])
            wait("tensor", "sM", sq_mark[(qq, it)])
            if it == 0 and qq > 0:
                wait("tensor", "vM", params_ps_read[qq - 1])
            par = it % 2
            opn("tensor", lambda t, i=it, P2=P: t.matmul(ps[4][0:1, 0:256], onesc[:, 0:1],
                Z1(P2, i), start=(i == 0), stop=(i == HT - 1)))
            opn("tensor", lambda t, i=it, p2=par: t.matmul(ps[5][0:1, 0:256], onesc[:, 0:1],
                zb8[:, 256 * p2:256 * (p2 + 1)], start=(i == 0), stop=(i == HT - 1)))

        sp_marks = {}
        for qq in range(NQ):
            P = qq % 2
            gq_ut = "gQ" if P == 0 else "gQb"
            if qq >= 2:
                wait("gpsimd", "tM", conv_end[qq - 2])
            for half in range(2):
                dma("gpsimd", gq_ut, utq[:, 8192 * P + 4096 * half:8192 * P + 4096 * (half + 1)].rearrange("p (jt s) -> p jt s", s=256),
                    usrc[:, 16 * half:16 * (half + 1), SQ * qq:SQ * (qq + 1)])
            ut_ld = cnt[gq_ut]
            wait("tensor", gq_ut, ut_ld)
            if qq == 0:
                wait("tensor", "tQ", tb_ld)
                wait("tensor", "vM", cnt["vM"])
            for it in range(HT):
                bank = it % 4
                rp = it % 2
                if qq > 0 or it >= 2:
                    key = (qq, it - 2) if it >= 2 else (qq - 1, HT - 2 + it)
                    wait("sync", "vM", evac_done[key])
                u_ld = dma("sync", "uQ%d" % rp, ursd[:, 256 * rp:256 * (rp + 1)],
                           uT[128 * it:128 * (it + 1), SQ * qq:SQ * (qq + 1)])
                if qq > 0 or it >= 4:
                    key = (qq, it - 4) if it >= 4 else (qq - 1, HT - 4 + it)
                    wait("tensor", "vM", evac_done[key])
                for jt in range(it + 1):
                    opn("tensor", lambda t, b=bank, j=jt, i=it, P2=P: t.matmul(
                        ps[b][:, 0:256], toepbig[:, 128 * (i - j):128 * (i - j) + 128], UQ(P2, j),
                        start=(j == 0), stop=(j == i)))
                if it >= 1:
                    emit_stats(qq, it - 1)
                mm_mark[(qq, it)] = mark("tensor", "tM")
                wait("vector", "tM", mm_mark[(qq, it)])
                wait("vector", "uQ%d" % rp, u_ld)
                op("vector", lambda v, i=it, rp2=rp, b=bank, P2=P: v.scalar_tensor_tensor(
                    out=Z1(P2, i), in0=ursd[:, 256 * rp2:256 * (rp2 + 1)], scalar=colp[:, 0:1],
                    in1=ps[b][:, 0:256], op0=ALU.mult, op1=ALU.add))
                evac_done[(qq, it)] = mark("vector", "vM")
                wait("scalar", "vM", evac_done[(qq, it)])
                op("scalar", lambda s_, i=it, p2=rp, P2=P: s_.activation(out=zb8[:, 256 * p2:256 * (p2 + 1)], in_=Z1(P2, i), func=AF.Square))
                sq_mark[(qq, it)] = mark("scalar", "sM")
            emit_stats(qq, HT - 1)
            conv_end[qq] = mark("tensor", "tM")
            wait("vector", "tM", conv_end[qq])
            op("vector", lambda v: v.tensor_scalar(out=rowf[0:1, 0:256], in0=ps[4][0:1, 0:256], scalar1=1.0 / 4096.0, scalar2=None, op0=ALU.mult))
            op("vector", lambda v: v.tensor_scalar(out=rowf[0:1, 512:768], in0=ps[5][0:1, 0:256], scalar1=1.0 / 4096.0, scalar2=None, op0=ALU.mult))
            params_ps_read[qq] = mark("vector", "vM")
            op("vector", lambda v: v.tensor_tensor(out=rowf[0:1, 1024:1280], in0=rowf[0:1, 0:256], in1=rowf[0:1, 0:256], op=ALU.mult))
            op("vector", lambda v: v.tensor_tensor(out=rowf[0:1, 1024:1280], in0=rowf[0:1, 512:768], in1=rowf[0:1, 1024:1280], op=ALU.subtract))
            op("vector", lambda v: v.tensor_scalar(out=rowf[0:1, 1024:1280], in0=rowf[0:1, 1024:1280], scalar1=1e-5, scalar2=None, op0=ALU.add))
            vm = mark("vector", "vM")
            wait("scalar", "vM", vm)
            op("scalar", lambda s_: s_.activation(out=rowf[0:1, 1280:1536], in_=rowf[0:1, 1024:1280], func=AF.Sqrt))
            sm = mark("scalar", "sM")
            wait("vector", "sM", sm)
            op("vector", lambda v: v.reciprocal(out=rowf[0:1, 1536:1792], in_=rowf[0:1, 1280:1536]))

            def hilo(v):
                v.tensor_copy(out=rowb[0:1, 0:256], in_=rowf[0:1, 0:256]); v.drain()
                v.tensor_copy(out=t4[0:1, 0:256], in_=rowb[0:1, 0:256]); v.drain()
                v.tensor_tensor(out=t4[0:1, 256:512], in0=rowf[0:1, 0:256], in1=t4[0:1, 0:256], op=ALU.subtract); v.drain()
                v.tensor_copy(out=rowb[0:1, 512:768], in_=t4[0:1, 256:512]); v.drain()
                v.tensor_copy(out=rowb[0:1, 1024:1280], in_=rowf[0:1, 1536:1792]); v.drain()
                v.tensor_copy(out=t4[0:1, 0:256], in_=rowb[0:1, 1024:1280]); v.drain()
                v.tensor_tensor(out=t4[0:1, 256:512], in0=rowf[0:1, 1536:1792], in1=t4[0:1, 0:256], op=ALU.subtract); v.drain()
                v.tensor_copy(out=rowb[0:1, 1536:1792], in_=t4[0:1, 256:512])
            op("vector", hilo)
            vm = mark("vector", "vM")
            wait("tensor", "vM", vm)
            opn("tensor", lambda t: t.matmul(ps[6][:, 0:256], ones1[0:1, :], rowb[0:1, 0:256], start=True, stop=False))
            opn("tensor", lambda t: t.matmul(ps[6][:, 0:256], ones1[0:1, :], rowb[0:1, 512:768], start=False, stop=True))
            opn("tensor", lambda t: t.matmul(ps[7][:, 0:256], ones1[0:1, :], rowb[0:1, 1024:1280], start=True, stop=False))
            opn("tensor", lambda t: t.matmul(ps[7][:, 0:256], ones1[0:1, :], rowb[0:1, 1536:1792], start=False, stop=True))
            tm = mark("tensor", "tM")
            wait("vector", "tM", tm)
            for g4 in range(8):
                nb = [t2, t3][g4 % 2]
                if g4 >= 2:
                    wait("vector", "aQ%d" % (g4 % 2), sp_marks[(qq, g4 - 2)])
                elif qq > 0:
                    wait("vector", "aQ%d" % (g4 % 2), sp_marks[(qq - 1, 6 + g4)])
                for j4 in range(4):
                    it = 4 * g4 + j4
                    op("vector", lambda v, i=it, j=j4, nb2=nb, P2=P: v.tensor_tensor(out=nb2[:, 256 * j:256 * (j + 1)], in0=Z1(P2, i), in1=ps[6][:, 0:256], op=ALU.subtract))
                    op("vector", lambda v, j=j4, nb2=nb: v.tensor_tensor(out=nb2[:, 256 * j:256 * (j + 1)], in0=nb2[:, 256 * j:256 * (j + 1)], in1=ps[7][:, 0:256], op=ALU.mult))
                vm = mark("vector", "vM")
                wait("scalar", "vM", vm)
                aqn = "aQ%d" % (g4 % 2)
                dma("scalar", aqn, x1spr[:, 4 * g4:4 * (g4 + 1), SQ * qq:SQ * (qq + 1)],
                    nb[:, 0:1024].rearrange("p (t s) -> p t s", s=256))
                sp_marks[(qq, g4)] = cnt[aqn]
                quantize(nb[:, 0:1024], 1, 0, False, qgr[:, 4 * g4:4 * (g4 + 1), SQ * qq:SQ * (qq + 1)], W=1024)
                if not shared_q:
                    quantize(nb[:, 0:1024], 2, 0, False, qur[:, 4 * g4:4 * (g4 + 1), SQ * qq:SQ * (qq + 1)], W=1024)

        gwr = gwT.rearrange("(ht p) i -> p ht i", p=128)
        upr = upT.rearrange("(ht p) i -> p ht i", p=128)
        GB = lambda par: utq[:, 8192 * par:8192 * par + 4096]
        UB = lambda par: utq[:, 8192 * par + 4096:8192 * (par + 1)]
        mm_done = {}
        bank_free = {}
        spill_done = {}
        ph1_v = cnt["vM"]
        for ic in range(IC):
            par = ic % 2
            if ic == 0:
                wait("gpsimd", "tM", conv_end[2])
            elif ic == 1:
                wait("gpsimd", "tM", conv_end[3])
            if ic >= 2:
                wait("gpsimd", "tM", mm_done[ic - 2])
            gqn = "gQ%d" % par
            for half in range(2):
                dma("gpsimd", gqn, GB(par).rearrange("p (ht i) -> p ht i", i=128)[:, 16 * half:16 * (half + 1)],
                    gwr[:, 16 * half:16 * (half + 1), 128 * ic:128 * (ic + 1)])
                dma("gpsimd", gqn, UB(par).rearrange("p (ht i) -> p ht i", i=128)[:, 16 * half:16 * (half + 1)],
                    upr[:, 16 * half:16 * (half + 1), 128 * ic:128 * (ic + 1)])
            w_ld = cnt[gqn]
            bb = 4 * (ic % 2)
            if ic == 0:
                wait("tensor", "vM", ph1_v)
            if ic >= 2:
                wait("tensor", "vM", bank_free[ic - 2])
            wait("tensor", gqn, w_ld)
            for t32 in range(HT):
                for h2 in range(2):
                    opn("tensor", lambda t, tt2=t32, h=h2, b=bb, p2=par: t.matmul(
                        ps[b + h][:, 0:512], GB(p2)[:, 128 * tt2:128 * (tt2 + 1)],
                        qg[:, S * tt2 + 512 * h:S * tt2 + 512 * (h + 1)],
                        start=(tt2 == 0), stop=(tt2 == HT - 1)))
                    opn("tensor", lambda t, tt2=t32, h=h2, b=bb, p2=par, qx=(qg if shared_q else qu): t.matmul(
                        ps[b + 2 + h][:, 0:512], UB(p2)[:, 128 * tt2:128 * (tt2 + 1)],
                        qx[:, S * tt2 + 512 * h:S * tt2 + 512 * (h + 1)],
                        start=(tt2 == 0), stop=(tt2 == HT - 1)))
            mm_done[ic] = mark("tensor", "tM")
            wait("vector", "tM", mm_done[ic])
            hp = ic % 2
            if ic == 0:
                wait("vector", "aQ0", cnt["aQ0"])
                wait("vector", "aQ1", cnt["aQ1"])
            if ic >= 2:
                wait("vector", "hsQ%d" % hp, spill_done[ic - 2])
            for h2 in range(2):
                op("vector", lambda v, h=h2, b=bb: v.tensor_scalar(out=t0[:, 512 * h:512 * (h + 1)], in0=ps[b + h][:, 0:512], scalar1=colp[:, 4:5], scalar2=None, op0=ALU.mult))
                op("vector", lambda v, h=h2, b=bb: v.tensor_scalar(out=t1[:, 512 * h:512 * (h + 1)], in0=ps[b + 2 + h][:, 0:512], scalar1=colp[:, 5:6], scalar2=None, op0=ALU.mult))
            bank_free[ic] = mark("vector", "vM")
            wait("scalar", "vM", bank_free[ic])
            op("scalar", lambda s_: s_.activation(out=t0[:, 0:512], in_=t0[:, 0:512], func=AF.Silu))
            op("scalar", lambda s_: s_.activation(out=t0[:, 512:1024], in_=t0[:, 512:1024], func=AF.Silu))
            sm = mark("scalar", "sM")
            wait("vector", "sM", sm)
            op("vector", lambda v: v.tensor_tensor(out=t3[:, 0:1024], in0=t0[:, 0:1024], in1=t1[:, 0:1024], op=ALU.mult))
            mark("vector", "vM")
            quantize(t3[:, 0:1024], 3, 3, True, hob[:, 1024 * hp:1024 * (hp + 1)], W=1024)
            wait("sync", "vM", cnt["vM"])
            spill_done[ic] = dma("sync", "hsQ%d" % hp, hidsp[128 * ic:128 * (ic + 1), :], hob[:, 1024 * hp:1024 * (hp + 1)])
            if debug == 1:
                spill_done[ic] = dma("sync", "hsQ%d" % hp, dbghid[128 * ic:128 * (ic + 1), :], hob[:, 1024 * hp:1024 * (hp + 1)])
        ph2a_v = cnt["vM"]

        NP2 = IC // 2
        NHH = 16
        dwr = dwT.rearrange("(b p) m -> p b m", p=128)
        hsr = hidsp.rearrange("(b p) s -> p b s", p=128)
        hq_eng = {0: "sync", 1: "sync", 2: "sync", 3: "scalar", 4: "scalar", 5: "scalar"}
        pair_mark = {}
        evict_mark = {}
        x1bufs = [t0, t1, t4, t5]
        x1_ld = {}
        dq_tgt = {}
        hq_tgt = {}

        def issue_pair(g):
            hh, p = divmod(g, NP2)
            sl = g % NSLOT
            if g >= NSLOT:
                wait("gpsimd", "tM", pair_mark[g - NSLOT])
            else:
                wait("gpsimd", "tM", mm_done[IC - 1])
            dq_tgt[g] = dma("gpsimd", "dQ%d" % sl,
                dbr[:, 512 * sl:512 * (sl + 1)].rearrange("p (b m) -> p b m", m=256),
                dwr[:, 2 * p:2 * p + 2, 256 * hh:256 * (hh + 1)])
            he = hq_eng[sl]
            if g >= NSLOT:
                wait(he, "tM", pair_mark[g - NSLOT])
            else:
                wait(he, "tM", mm_done[IC - 1])
            if hh == 0:
                wait(he, "hsQ0", cnt["hsQ0"])
                wait(he, "hsQ1", cnt["hsQ1"])
            hq_tgt[g] = dma(he, "hQ%d" % sl,
                hbr[:, 2048 * sl:2048 * (sl + 1)].rearrange("p (b s) -> p b s", s=S),
                hsr[:, 2 * p:2 * p + 2, :])

        def issue_x1(it):
            sl = it % 4
            if it < 4:
                wait("scalar", "vM", ph2a_v)
                wait("scalar", "aQ0", cnt["aQ0"])
                wait("scalar", "aQ1", cnt["aQ1"])
            else:
                wait("scalar", "vM", evict_mark[(it - 4) // 2])
            x1_ld[it] = dma("scalar", "xQ%d" % sl, x1bufs[sl][:, 0:1024],
                            x1sp[128 * it:128 * (it + 1), :])

        NPREF = NSLOT
        for g0 in range(NPREF):
            issue_pair(g0)
        for it0 in range(4):
            issue_x1(it0)
        for hh in range(NHH):
            bb = 4 * (hh % 2)
            if hh == 0:
                wait("tensor", "vM", bank_free[IC - 2])
            elif hh == 1:
                wait("tensor", "vM", bank_free[IC - 1])
            else:
                wait("tensor", "vM", evict_mark[hh - 2])
            for p in range(NP2):
                g = hh * NP2 + p
                sl = g % NSLOT
                wait("tensor", "dQ%d" % sl, dq_tgt[g])
                wait("tensor", "hQ%d" % sl, hq_tgt[g])
                for icq in range(2):
                    ic = 2 * p + icq
                    for f2 in range(2):
                        for h2 in range(2):
                            opn("tensor", lambda t, s2=sl, q=icq, f=f2, h=h2, b=bb, i=ic: t.matmul(
                                ps[b + 2 * f + h][:, 0:512],
                                dbr[:, 512 * s2 + 256 * q + 128 * f:512 * s2 + 256 * q + 128 * (f + 1)],
                                hbr[:, 2048 * s2 + 1024 * q + 512 * h:2048 * s2 + 1024 * q + 512 * (h + 1)],
                                start=(i == 0), stop=(i == IC - 1)))
                pair_mark[g] = mark("tensor", "tM")
                if g + NPREF < NHH * NP2:
                    issue_pair(g + NPREF)
            wait("vector", "tM", pair_mark[hh * NP2 + NP2 - 1])
            for f2 in range(2):
                it = 2 * hh + f2
                wait("vector", "xQ%d" % (it % 4), x1_ld[it])
                for h2 in range(2):
                    op("vector", lambda v, f=f2, h=h2, b=bb, i=it: v.scalar_tensor_tensor(
                        out=z2b[:, 1024 * i + 512 * h:1024 * i + 512 * (h + 1)],
                        in0=ps[b + 2 * f + h][:, 0:512], scalar=colp[:, 6:7],
                        in1=x1bufs[i % 4][:, 512 * h:512 * (h + 1)],
                        op0=ALU.mult, op1=ALU.add))
            evict_mark[hh] = mark("vector", "vM")
            if 2 * hh + 4 < HT:
                issue_x1(2 * hh + 4)
            if 2 * hh + 5 < HT:
                issue_x1(2 * hh + 5)

        wait("scalar", "vM", evict_mark[NHH - 1])
        wait("tensor", "vM", evict_mark[NHH - 1])
        sqm = {}
        stat_m = {}
        for it in range(HT):
            sl2 = it % 2
            if it >= 2:
                wait("scalar", "tM", stat_m[it - 2])
            op("scalar", lambda s_, i=it, s2=sl2: s_.activation(
                out=zb8[:, 1024 * s2:1024 * (s2 + 1)],
                in_=z2b[:, 1024 * i:1024 * (i + 1)], func=AF.Square))
            sqm[it] = mark("scalar", "sM")
            wait("tensor", "sM", sqm[it])
            for h2 in range(2):
                opn("tensor", lambda t, h=h2, i=it: t.matmul(
                    ps[h][0:1, 0:512], onesc[:, 0:1],
                    z2b[:, 1024 * i + 512 * h:1024 * i + 512 * (h + 1)],
                    start=(i == 0), stop=(i == HT - 1)))
                opn("tensor", lambda t, h=h2, i=it, s2=sl2: t.matmul(
                    ps[2 + h][0:1, 0:512], onesc[:, 0:1],
                    zb8[:, 1024 * s2 + 512 * h:1024 * s2 + 512 * (h + 1)],
                    start=(i == 0), stop=(i == HT - 1)))
            stat_m[it] = mark("tensor", "tM")
        wait("vector", "tM", cnt["tM"])
        for h2 in range(2):
            op("vector", lambda v, h=h2: v.tensor_scalar(out=rowf[0:1, 512 * h:512 * (h + 1)], in0=ps[h][0:1, 0:512], scalar1=1.0 / 4096.0, scalar2=None, op0=ALU.mult))
            op("vector", lambda v, h=h2: v.tensor_scalar(out=t4[0:1, 0:512], in0=ps[2 + h][0:1, 0:512], scalar1=1.0 / 4096.0, scalar2=None, op0=ALU.mult))
            op("vector", lambda v, h=h2: v.tensor_tensor(out=t4[0:1, 512:1024], in0=rowf[0:1, 512 * h:512 * (h + 1)], in1=rowf[0:1, 512 * h:512 * (h + 1)], op=ALU.mult))
            op("vector", lambda v: v.tensor_tensor(out=t4[0:1, 0:512], in0=t4[0:1, 0:512], in1=t4[0:1, 512:1024], op=ALU.subtract))
            op("vector", lambda v: v.tensor_scalar(out=t4[0:1, 0:512], in0=t4[0:1, 0:512], scalar1=1e-5, scalar2=None, op0=ALU.add))
            vm = mark("vector", "vM")
            wait("scalar", "vM", vm)
            op("scalar", lambda s_: s_.activation(out=t4[0:1, 512:1024], in_=t4[0:1, 0:512], func=AF.Sqrt))
            sm = mark("scalar", "sM")
            wait("vector", "sM", sm)
            op("vector", lambda v, h=h2: v.reciprocal(out=rowf[0:1, 1024 + 512 * h:1024 + 512 * (h + 1)], in_=t4[0:1, 512:1024]))

        def hilo2(v):
            v.tensor_copy(out=rowb[0:1, 0:1024], in_=rowf[0:1, 0:1024]); v.drain()
            v.tensor_copy(out=t4[0:1, 0:1024], in_=rowb[0:1, 0:1024]); v.drain()
            v.tensor_tensor(out=t5[0:1, 0:1024], in0=rowf[0:1, 0:1024], in1=t4[0:1, 0:1024], op=ALU.subtract); v.drain()
            v.tensor_copy(out=rowb[0:1, 1024:2048], in_=t5[0:1, 0:1024]); v.drain()
            v.tensor_copy(out=rowb[0:1, 2048:3072], in_=rowf[0:1, 1024:2048]); v.drain()
            v.tensor_copy(out=t4[0:1, 0:1024], in_=rowb[0:1, 2048:3072]); v.drain()
            v.tensor_tensor(out=t5[0:1, 0:1024], in0=rowf[0:1, 1024:2048], in1=t4[0:1, 0:1024], op=ALU.subtract); v.drain()
            v.tensor_copy(out=rowb[0:1, 3072:4096], in_=t5[0:1, 0:1024])
        op("vector", hilo2)
        vm = mark("vector", "vM")
        wait("tensor", "vM", vm)
        for h2 in range(2):
            opn("tensor", lambda t, h=h2: t.matmul(ps[4 + h][:, 0:512], ones1[0:1, :], rowb[0:1, 512 * h:512 * (h + 1)], start=True, stop=False))
            opn("tensor", lambda t, h=h2: t.matmul(ps[4 + h][:, 0:512], ones1[0:1, :], rowb[0:1, 1024 + 512 * h:1024 + 512 * (h + 1)], start=False, stop=True))
            opn("tensor", lambda t, h=h2: t.matmul(ps[6 + h][:, 0:512], ones1[0:1, :], rowb[0:1, 2048 + 512 * h:2048 + 512 * (h + 1)], start=True, stop=False))
            opn("tensor", lambda t, h=h2: t.matmul(ps[6 + h][:, 0:512], ones1[0:1, :], rowb[0:1, 3072 + 512 * h:3072 + 512 * (h + 1)], start=False, stop=True))
        tm = mark("tensor", "tM")
        wait("vector", "tM", tm)
        o_stm = {}
        for it in range(HT):
            buf = [t3, t2][it % 2]
            oqn = "oQ" if it % 2 == 0 else "oQb"
            if it >= 2:
                wait("vector", oqn, o_stm[it - 2])
            for h2 in range(2):
                op("vector", lambda v, h=h2, i=it, b=buf: v.tensor_tensor(
                    out=b[:, 512 * h:512 * (h + 1)],
                    in0=z2b[:, 1024 * i + 512 * h:1024 * i + 512 * (h + 1)],
                    in1=ps[4 + h][:, 0:512], op=ALU.subtract))
                op("vector", lambda v, h=h2, b=buf: v.tensor_tensor(
                    out=b[:, 512 * h:512 * (h + 1)], in0=b[:, 512 * h:512 * (h + 1)],
                    in1=ps[6 + h][:, 0:512], op=ALU.mult))
            vm = mark("vector", "vM")
            wait("sync", "vM", vm)
            o_stm[it] = dma("sync", oqn, outT[128 * it:128 * (it + 1), :], buf[:, 0:1024])
        wait("tensor", "oQ", cnt["oQ"])
        wait("tensor", "oQb", cnt["oQb"])
        wait("sync", "oQ", cnt["oQ"])
        wait("sync", "oQb", cnt["oQb"])

        @block.sync
        def _(e):
            for f in ops["sync"]:
                f(e)

        @block.scalar
        def _(e):
            for f in ops["scalar"]:
                f(e)

        @block.vector
        def _(e):
            for f in ops["vector"]:
                f(e)

        @block.gpsimd
        def _(e):
            for f in ops["gpsimd"]:
                f(e)

        @block.tensor
        def _(e):
            for f in ops["tensor"]:
                f(e)

    return nc


_NC_CACHE = {}


def kernel(L0_attn_u, L0_attn_Lambda_ri, L0_attn_p_ri, L0_attn_q_ri, L0_attn_Vc_ri,
           L0_attn_Ct_ri, L0_attn_Bp, L0_attn_D, L0_attn_log_step,
           L0_ffn_gate_w, L0_ffn_up_w, L0_ffn_down_w,
           L0_ffn_gate_w_scale, L0_ffn_up_w_scale, L0_ffn_down_w_scale,
           L0_ffn_gate_in_scale, L0_ffn_up_in_scale, L0_ffn_down_in_scale):
    global _NC_CACHE
    u = np.asarray(L0_attn_u, dtype=np.float32).reshape(8192, 4096)
    gwTh = np.ascontiguousarray(np.asarray(L0_ffn_gate_w, dtype=np.float32).T)
    upTh = np.ascontiguousarray(np.asarray(L0_ffn_up_w, dtype=np.float32).T)
    dwTh = np.ascontiguousarray(np.asarray(L0_ffn_down_w, dtype=np.float32).T)
    sca = np.array([[np.float32(np.asarray(L0_attn_D).reshape(-1)[0]),
                     np.float32(np.asarray(L0_attn_log_step).reshape(-1)[0]),
                     np.float32(L0_ffn_gate_w_scale), np.float32(L0_ffn_up_w_scale),
                     np.float32(L0_ffn_down_w_scale), np.float32(L0_ffn_gate_in_scale),
                     np.float32(L0_ffn_up_in_scale), np.float32(L0_ffn_down_in_scale)]],
                   dtype=np.float32)
    common = {
        "gwT": gwTh, "upT": upTh, "dwT": dwTh,
        "lam": np.ascontiguousarray(np.asarray(L0_attn_Lambda_ri, dtype=np.float32)),
        "pri": np.ascontiguousarray(np.asarray(L0_attn_p_ri, dtype=np.float32)),
        "qri": np.ascontiguousarray(np.asarray(L0_attn_q_ri, dtype=np.float32)),
        "vcr": np.ascontiguousarray(np.asarray(L0_attn_Vc_ri, dtype=np.float32)),
        "ctr": np.ascontiguousarray(np.asarray(L0_attn_Ct_ri, dtype=np.float32)),
        "bp": np.ascontiguousarray(np.asarray(L0_attn_Bp, dtype=np.float32)),
        "sca": sca,
    }
    in_maps = []
    for c in range(8):
        m = dict(common)
        m["uT"] = np.ascontiguousarray(u[1024 * c:1024 * (c + 1), :].T)
        in_maps.append(m)
    shared = np.float32(L0_ffn_gate_in_scale) == np.float32(L0_ffn_up_in_scale)
    if shared not in _NC_CACHE:
        _NC_CACHE[shared] = build_program(shared_q=shared)
    res = run_bass_kernel_spmd(_NC_CACHE[shared], in_maps, core_ids=list(range(8)))
    out = np.empty((8192, 4096), dtype=np.float32)
    for c in range(8):
        out[1024 * c:1024 * (c + 1), :] = res.results[c]["outT"].T
    return out.reshape(1, 8192, 4096)


if __name__ == "__main__":
    pass


# version 14
# speedup vs baseline: 1.0308x; 1.0308x over previous
import numpy as np
import ml_dtypes
import concourse.bass as bass
import concourse.mybir as mybir
from concourse.bass_utils import run_bass_kernel_spmd
from contextlib import ExitStack

F32 = mybir.dt.float32
BF16 = mybir.dt.bfloat16
FP8 = mybir.dt.float8e4
I32 = mybir.dt.int32
AF = mybir.ActivationFunctionType
ALU = mybir.AluOpType

LN2C = float(np.log(2.0))
PI = float(np.pi)

S = 1024
SQ = 256
NQ = S // SQ
HT = 32
IC = 86
DM = 4096
DI = 11008


def build_program(debug=False, shared_q=True):
    nc = bass.Bass("TRN2", target_bir_lowering=False, debug=False)

    uT = nc.dram_tensor("uT", [DM, S], F32, kind="ExternalInput").ap()
    gwT = nc.dram_tensor("gwT", [DM, DI], F32, kind="ExternalInput").ap()
    upT = nc.dram_tensor("upT", [DM, DI], F32, kind="ExternalInput").ap()
    dwT = nc.dram_tensor("dwT", [DI, DM], F32, kind="ExternalInput").ap()
    lam = nc.dram_tensor("lam", [64, 2], F32, kind="ExternalInput").ap()
    pri = nc.dram_tensor("pri", [64, 2], F32, kind="ExternalInput").ap()
    qri = nc.dram_tensor("qri", [64, 2], F32, kind="ExternalInput").ap()
    vcr = nc.dram_tensor("vcr", [64, 64, 2], F32, kind="ExternalInput").ap()
    ctr = nc.dram_tensor("ctr", [1, 64, 2], F32, kind="ExternalInput").ap()
    bp = nc.dram_tensor("bp", [64, 1], F32, kind="ExternalInput").ap()
    sca = nc.dram_tensor("sca", [1, 8], F32, kind="ExternalInput").ap()

    outT = nc.dram_tensor("outT", [DM, S], F32, kind="ExternalOutput").ap()
    if debug == 2:
        dbgx1 = nc.dram_tensor("dbgx1", [DM, S], F32, kind="ExternalOutput").ap()
        dbghid = nc.dram_tensor("dbghid", [DI, S], FP8, kind="ExternalOutput").ap()
        dbgz2 = nc.dram_tensor("dbgz2", [DM, S], F32, kind="ExternalOutput").ap()
    if debug == 1:
        dbgK = nc.dram_tensor("dbgK", [1, 8192], F32, kind="ExternalOutput").ap()
        dbgz1 = nc.dram_tensor("dbgz1", [DM, SQ], F32, kind="ExternalOutput").ap()
        dbgx1 = nc.dram_tensor("dbgx1", [DM, S], F32, kind="ExternalOutput").ap()
        dbgqg = nc.dram_tensor("dbgqg", [128, HT * S], FP8, kind="ExternalOutput").ap()
        dbghid = nc.dram_tensor("dbghid", [DI, S], FP8, kind="ExternalOutput").ap()
        dbgz2 = nc.dram_tensor("dbgz2", [DM, S], F32, kind="ExternalOutput").ap()

    kmat = nc.dram_tensor("kmat", [128, 8320], BF16, kind="Internal").ap()
    x1sp = nc.dram_tensor("x1sp", [DM, S], F32, kind="Internal").ap()
    hidsp = nc.dram_tensor("hidsp", [DI, S], FP8, kind="Internal").ap()

    with ExitStack() as ctx:
        sb = lambda nm, shp, dt: ctx.enter_context(nc.sbuf_tensor(nm, shp, dt))
        arena = sb("arena", [128, 2 * HT * S], FP8)
        qg = arena.ap()[:, 0:HT * S]
        qu = arena.ap()[:, HT * S:2 * HT * S]
        z2b = arena.ap().bitcast(BF16)
        utq = sb("utq", [128, 16384], BF16)
        z1b = sb("z1b", [128, 16384], BF16)
        NSLOT = 8
        dbr = utq.ap()[:, 0:512 * NSLOT]
        hbr = utq.ap().bitcast(FP8)[:, 8192:8192 + 2048 * NSLOT]
        hob = sb("hob", [128, 2048], FP8)
        t0 = sb("t0", [128, 1024], F32)
        t1 = sb("t1", [128, 1024], F32)
        t2 = sb("t2", [128, 1024], F32)
        t3 = sb("t3", [128, 1024], F32)
        t4 = sb("t4", [128, 1024], F32)
        t5 = sb("t5", [128, 1024], F32)
        ti = sb("ti", [128, 1024], I32)
        zb8 = sb("zb8", [128, 2048], BF16)
        toepbig = sb("toepbig", [128, 4096], BF16)
        ursd = sb("ursd", [128, 512], F32)
        ksm = sb("ksm", [64, 2048], F32)
        kbb = sb("kbb", [64, 1024], BF16)
        vcs = sb("vcs", [64, 128], F32)
        vcb = sb("vcb", [64, 128], BF16)
        smc = sb("smc", [64, 16], F32)
        v8b = sb("v8b", [64, 8], BF16)
        colp = sb("colp", [128, 16], F32)
        rowf = sb("rowf", [1, 2048], F32)
        rowb = sb("rowb", [1, 4096], BF16)
        onesc = sb("onesc", [128, 1], BF16)
        ones1 = sb("ones1", [1, 128], BF16)
        idn = sb("idn", [128, 128], BF16)
        biasp = sb("biasp", [128, 8], F32)
        iot = sb("iot", [128, 64], I32)
        mrow = sb("mrow", [1, 64], BF16)
        scs = sb("scs", [1, 36], F32)

        ps = [ctx.enter_context(nc.psum_tensor(f"ps{k}", [128, 512], F32)) for k in range(8)]

        sem = lambda nm: ctx.enter_context(nc.semaphore(nm))
        sems = {nm: sem(nm) for nm in ("sQ", "aQ", "gQ", "tM", "vM", "sM", "gM", "oQ",
                               "tQ", "uQ0", "uQ1", "aQ0", "aQ1", "hsQ0", "hsQ1",
                               "gQ0", "gQ1", "gQb", "oQb", "hQ0", "hQ1", "hQ2", "hQ3",
                               "dQ0", "dQ1", "dQ2", "dQ3", "dQ4", "dQ5", "dQ6", "dQ7",
                               "hQ4", "hQ5", "hQ6", "hQ7",
                               "xQ0", "xQ1", "xQ2", "xQ3")}

        block = ctx.enter_context(nc.Block())

        ops = {e: [] for e in ("sync", "scalar", "vector", "gpsimd", "tensor")}
        cnt = {k: 0 for k in sems}

        def emit(eng, fn):
            ops[eng].append(fn)

        def dma(eng, cname, out, in_, slow=False):
            cnt[cname] += 16
            s_ = sems[cname]
            emit(eng, lambda e, o=out, i=in_, s2=s_, sl=slow: e.dma_start(out=o, in_=i, allow_slow_non_contiguous=sl).then_inc(s2, 16))
            return cnt[cname]

        def wait(eng, cname, val=None):
            v = cnt[cname] if val is None else val
            s_ = sems[cname]
            emit(eng, lambda e, s2=s_, v2=v: e.wait_ge(s2, v2))

        last = {e: None for e in ops}

        def mark(eng, cname):
            cnt[cname] += 1
            s_ = sems[cname]
            def run(e, s2=s_, eng2=eng):
                li = last[eng2]
                if li is not None:
                    li.then_inc(s2, 1)
                    last[eng2] = None
                else:
                    e.drain()
                    e.sem_inc(s2, 1)
            emit(eng, run)
            return cnt[cname]

        def mark_hard(eng, cname):
            cnt[cname] += 1
            s_ = sems[cname]
            def run(e, s2=s_, eng2=eng):
                last[eng2] = None
                e.drain()
                e.sem_inc(s2, 1)
            emit(eng, run)
            return cnt[cname]

        def op(eng, fn):
            def run(e, f=fn, eng2=eng):
                last[eng2] = f(e)
                e.drain()
            emit(eng, run)

        def opn(eng, fn):
            def run(e, f=fn, eng2=eng):
                last[eng2] = f(e)
            emit(eng, run)

        KS = lambda k2: ksm[:, 64 * k2:64 * (k2 + 1)]

        dma("sync", "sQ", smc[:, 0:1], lam[:, 0:1], slow=True)
        dma("sync", "sQ", smc[:, 1:2], lam[:, 1:2], slow=True)
        dma("sync", "sQ", smc[:, 2:3], pri[:, 0:1], slow=True)
        dma("sync", "sQ", smc[:, 3:4], pri[:, 1:2], slow=True)
        dma("sync", "sQ", smc[:, 4:5], qri[:, 0:1], slow=True)
        dma("sync", "sQ", smc[:, 5:6], qri[:, 1:2], slow=True)
        dma("sync", "sQ", smc[:, 6:7], ctr[0, :, 0:1], slow=True)
        dma("sync", "sQ", smc[:, 7:8], ctr[0, :, 1:2], slow=True)
        dma("sync", "sQ", smc[:, 8:9], bp)
        dma("sync", "sQ", vcs[:, 0:64], vcr[:, :, 0], slow=True)
        dma("sync", "sQ", vcs[:, 64:128], vcr[:, :, 1], slow=True)
        dma("sync", "sQ", scs[0:1, 0:8], sca)
        small_loads = cnt["sQ"]

        op("vector", lambda v: v.memset(onesc[:], 1.0))
        op("vector", lambda v: v.memset(ones1[:], 1.0))
        op("vector", lambda v: v.memset(biasp[:, 0:1], -3.0 * LN2C))
        op("vector", lambda v: v.memset(biasp[:, 1:2], 3.0 * LN2C))
        op("vector", lambda v: v.memset(biasp[:, 2:3], PI / 2.0))
        op("vector", lambda v: v.memset(biasp[:, 3:4], -4.0 * LN2C))
        op("vector", lambda v: v.memset(biasp[:, 4:5], 1e-38))

        op("gpsimd", lambda g: g.iota(iot[:, 0:64], pattern=[[-1, 64]], base=0, channel_multiplier=1))
        gm_i1 = mark("gpsimd", "gM")
        wait("vector", "gM", gm_i1)
        op("vector", lambda v: v.tensor_scalar(out=idn[:, 0:64], in0=iot[:, 0:64], scalar1=0, scalar2=None, op0=ALU.is_equal))
        vm_iot1 = mark("vector", "vM")
        wait("gpsimd", "vM", vm_iot1)
        op("gpsimd", lambda g: g.iota(iot[:, 0:64], pattern=[[-1, 64]], base=-64, channel_multiplier=1))
        gm_i2 = mark("gpsimd", "gM")
        wait("vector", "gM", gm_i2)
        op("vector", lambda v: v.tensor_scalar(out=idn[:, 64:128], in0=iot[:, 0:64], scalar1=0, scalar2=None, op0=ALU.is_equal))
        mark("vector", "vM")
        vm_idn = cnt["vM"]

        wait("scalar", "sQ", small_loads)
        op("scalar", lambda s_: s_.activation(out=scs[0:1, 16:17], in_=scs[0:1, 1:2], func=AF.Exp))
        sm_step = mark("scalar", "sM")
        wait("vector", "sM", sm_step)
        wait("vector", "sQ", small_loads)
        op("vector", lambda v: v.tensor_scalar(out=scs[0:1, 8:9], in0=scs[0:1, 0:1], scalar1=1.0, scalar2=None, op0=ALU.add))
        op("vector", lambda v: v.reciprocal(out=scs[0:1, 17:18], in_=scs[0:1, 16:17]))
        op("vector", lambda v: v.tensor_scalar(out=scs[0:1, 9:10], in0=scs[0:1, 17:18], scalar1=2.0, scalar2=None, op0=ALU.mult))
        op("vector", lambda v: v.reciprocal(out=scs[0:1, 10:11], in_=scs[0:1, 5:6]))
        op("vector", lambda v: v.reciprocal(out=scs[0:1, 11:12], in_=scs[0:1, 6:7]))
        op("vector", lambda v: v.reciprocal(out=scs[0:1, 12:13], in_=scs[0:1, 7:8]))
        op("vector", lambda v: v.tensor_tensor(out=scs[0:1, 13:14], in0=scs[0:1, 2:3], in1=scs[0:1, 5:6], op=ALU.mult))
        op("vector", lambda v: v.tensor_tensor(out=scs[0:1, 14:15], in0=scs[0:1, 3:4], in1=scs[0:1, 6:7], op=ALU.mult))
        op("vector", lambda v: v.tensor_tensor(out=scs[0:1, 18:19], in0=scs[0:1, 4:5], in1=scs[0:1, 7:8], op=ALU.mult))
        op("vector", lambda v: v.tensor_scalar(out=scs[0:1, 15:16], in0=scs[0:1, 18:19], scalar1=2.0, scalar2=None, op0=ALU.mult))
        op("vector", lambda v: v.tensor_copy(out=rowb[0:1, 0:16], in_=scs[0:1, 0:16]))
        op("vector", lambda v: v.tensor_copy(out=scs[0:1, 20:36], in_=rowb[0:1, 0:16]))
        op("vector", lambda v: v.tensor_tensor(out=scs[0:1, 20:36], in0=scs[0:1, 0:16], in1=scs[0:1, 20:36], op=ALU.subtract))
        op("vector", lambda v: v.tensor_copy(out=rowb[0:1, 16:32], in_=scs[0:1, 20:36]))
        vm_scal = mark("vector", "vM")

        wait("tensor", "vM", vm_scal)
        for dst, src in [(0, 8), (1, 10), (2, 11), (3, 12), (4, 13), (5, 14), (6, 15)]:
            opn("tensor", lambda t, s2=src: t.matmul(ps[0][:, 0:1], ones1[0:1, :], rowb[0:1, s2:s2 + 1], start=True, stop=False))
            opn("tensor", lambda t, s2=src: t.matmul(ps[0][:, 0:1], ones1[0:1, :], rowb[0:1, 16 + s2:17 + s2], start=False, stop=True))
            tm = mark("tensor", "tM")
            wait("vector", "tM", tm)
            op("vector", lambda v, d=dst: v.tensor_copy(out=colp[:, d:d + 1], in_=ps[0][:, 0:1]))
            vm = mark("vector", "vM")
            wait("tensor", "vM", vm)
        opn("tensor", lambda t: t.matmul(ps[1][0:64, 0:1], ones1[0:1, 0:64], rowb[0:1, 9:10], start=True, stop=False))
        opn("tensor", lambda t: t.matmul(ps[1][0:64, 0:1], ones1[0:1, 0:64], rowb[0:1, 25:26], start=False, stop=True))
        tm = mark("tensor", "tM")
        wait("vector", "tM", tm)
        op("vector", lambda v: v.tensor_copy(out=smc[:, 10:11], in_=ps[1][0:64, 0:1]))
        op("vector", lambda v: v.tensor_scalar(out=smc[:, 9:10], in0=smc[:, 0:1], scalar1=-1.0, scalar2=None, op0=ALU.mult))
        vm_cols = mark("vector", "vM")

        op("vector", lambda v: v.tensor_copy(out=vcb[:, 0:64], in_=vcs[:, 0:64]))
        op("vector", lambda v: v.tensor_copy(out=vcb[:, 64:128], in_=vcs[:, 64:128]))
        op("vector", lambda v: v.tensor_copy(out=kbb[:, 0:1], in_=smc[:, 8:9]))
        vm = mark("vector", "vM")
        wait("tensor", "vM", vm)
        wait("tensor", "vM", vm_idn)
        opn("tensor", lambda t: t.transpose(ps[2][0:64, 0:32].bitcast(BF16), vcb[:, 0:64], idn[0:64, 0:64]))
        opn("tensor", lambda t: t.transpose(ps[3][0:64, 0:32].bitcast(BF16), vcb[:, 64:128], idn[0:64, 0:64]))
        tm = mark("tensor", "tM")
        wait("vector", "tM", tm)
        op("vector", lambda v: v.tensor_copy(out=vcb[:, 0:64], in_=ps[2][0:64, 0:32].bitcast(BF16)))
        op("vector", lambda v: v.tensor_copy(out=vcb[:, 64:128], in_=ps[3][0:64, 0:32].bitcast(BF16)))
        vm = mark("vector", "vM")
        wait("tensor", "vM", vm)
        opn("tensor", lambda t: t.matmul(ps[2][0:64, 0:1], vcb[:, 0:64], kbb[:, 0:1], start=True, stop=True))
        opn("tensor", lambda t: t.matmul(ps[3][0:64, 0:1], vcb[:, 64:128], kbb[:, 0:1], start=True, stop=True))
        tm = mark("tensor", "tM")
        wait("vector", "tM", tm)
        op("vector", lambda v: v.tensor_copy(out=smc[:, 11:12], in_=ps[2][0:64, 0:1]))
        op("vector", lambda v: v.tensor_copy(out=smc[:, 12:13], in_=ps[3][0:64, 0:1]))

        def cmul_conj(v, are, aim, bre, bim, outre, outim):
            v.tensor_tensor(out=t5[0:64, 0:1], in0=smc[:, are:are + 1], in1=smc[:, bre:bre + 1], op=ALU.mult); v.drain()
            v.tensor_tensor(out=t5[0:64, 1:2], in0=smc[:, aim:aim + 1], in1=smc[:, bim:bim + 1], op=ALU.mult); v.drain()
            v.tensor_tensor(out=t5[0:64, 4:5], in0=t5[0:64, 0:1], in1=t5[0:64, 1:2], op=ALU.add); v.drain()
            v.tensor_tensor(out=t5[0:64, 2:3], in0=smc[:, are:are + 1], in1=smc[:, bim:bim + 1], op=ALU.mult); v.drain()
            v.tensor_tensor(out=t5[0:64, 3:4], in0=smc[:, aim:aim + 1], in1=smc[:, bre:bre + 1], op=ALU.mult); v.drain()
            v.tensor_tensor(out=t5[0:64, 5:6], in0=t5[0:64, 2:3], in1=t5[0:64, 3:4], op=ALU.subtract); v.drain()
            v.tensor_copy(out=v8b[:, outre:outre + 1], in_=t5[0:64, 4:5]); v.drain()
            v.tensor_copy(out=v8b[:, outim:outim + 1], in_=t5[0:64, 5:6])
        op("vector", lambda v: cmul_conj(v, 6, 7, 11, 12, 0, 1))
        op("vector", lambda v: cmul_conj(v, 6, 7, 2, 3, 2, 3))
        op("vector", lambda v: cmul_conj(v, 4, 5, 11, 12, 4, 5))
        op("vector", lambda v: cmul_conj(v, 4, 5, 2, 3, 6, 7))
        vm_v8 = mark("vector", "vM")

        zco = t0[0:64, 0:512]
        zsi = t0[0:64, 512:1024]
        zdre = t1[0:64, 0:512]
        zdim = t1[0:64, 512:1024]
        zt = t2[0:64, 0:512]
        zt2 = t2[0:64, 512:1024]
        zrec = t3[0:64, 0:512]
        zji = t3[0:64, 512:1024]
        crebf = zb8[0:64, 0:512]
        cimbf = zb8[0:64, 512:1024]
        for cc4 in range(8):
            op("gpsimd", lambda g, c=cc4: g.iota(zji, pattern=[[1, 512]], base=512 * c, channel_multiplier=0, allow_small_or_imprecise_dtypes=True))
            gm = mark("gpsimd", "gM")
            wait("scalar", "gM", gm)
            if cc4 == 0:
                wait("scalar", "vM", vm_cols)
            op("scalar", lambda s_: s_.activation(out=zco, in_=zji, func=AF.Sin, scale=-PI / 4096.0, bias=biasp[0:64, 2:3]))
            op("scalar", lambda s_: s_.activation(out=zsi, in_=zji, func=AF.Sin, scale=PI / 4096.0))
            sm = mark("scalar", "sM")
            wait("gpsimd", "sM", sm)
            wait("vector", "sM", sm)
            op("vector", lambda v: v.tensor_scalar(out=zdre, in0=zco, scalar1=smc[:, 9:10], scalar2=None, op0=ALU.mult))
            op("vector", lambda v: v.tensor_scalar(out=zt, in0=zsi, scalar1=smc[:, 10:11], scalar2=None, op0=ALU.mult))
            op("vector", lambda v: v.tensor_scalar(out=zt2, in0=zco, scalar1=smc[:, 1:2], scalar2=None, op0=ALU.mult))
            op("vector", lambda v: v.tensor_tensor(out=zdim, in0=zt, in1=zt2, op=ALU.subtract))
            vm = mark("vector", "vM")
            wait("scalar", "vM", vm)
            op("scalar", lambda s_: s_.activation(out=zt, in_=zdre, func=AF.Square))
            op("scalar", lambda s_: s_.activation(out=zt2, in_=zdim, func=AF.Square))
            sm = mark("scalar", "sM")
            wait("vector", "sM", sm)
            op("vector", lambda v: v.tensor_tensor(out=zrec, in0=zt, in1=zt2, op=ALU.add))
            op("vector", lambda v: v.reciprocal(out=zrec, in_=zrec))
            op("vector", lambda v: v.tensor_tensor(out=crebf, in0=zdre, in1=zrec, op=ALU.mult))
            op("vector", lambda v: v.tensor_scalar(out=zt, in0=zdim, scalar1=-1.0, scalar2=None, op0=ALU.mult))
            op("vector", lambda v: v.tensor_tensor(out=cimbf, in0=zt, in1=zrec, op=ALU.mult))
            vm = mark("vector", "vM")
            wait("tensor", "vM", vm)
            if cc4 == 0:
                wait("tensor", "vM", vm_v8)
            for sc16 in range(8):
                q64 = 8 * cc4 + sc16
                opn("tensor", lambda t, s2=sc16, q2=q64: t.matmul(ps[4][0:64, 8 * q2:8 * q2 + 8], crebf[:, 64 * s2:64 * s2 + 64], v8b[:, 0:8], start=True, stop=True))
                opn("tensor", lambda t, s2=sc16, q2=q64: t.matmul(ps[5][0:64, 8 * q2:8 * q2 + 8], cimbf[:, 64 * s2:64 * s2 + 64], v8b[:, 0:8], start=True, stop=True))
            tm = mark("tensor", "tM")
            wait("vector", "tM", tm)
        op("vector", lambda v: v.tensor_copy(out=t4[0:64, 0:512], in_=ps[4][0:64, 0:512]))
        op("vector", lambda v: v.tensor_copy(out=t4[0:64, 512:1024], in_=ps[5][0:64, 0:512]))
        mA = t4[0:64, 0:512]
        mB = t4[0:64, 512:1024]
        for ra, rb, sign, dst in [(0, 1, -1, 0), (1, 0, +1, 1), (2, 3, -1, 2), (3, 2, +1, 3),
                                  (4, 5, -1, 4), (5, 4, +1, 5), (6, 7, -1, 6), (7, 6, +1, 7)]:
            aop = ALU.subtract if sign < 0 else ALU.add
            op("vector", lambda v, a=ra, b=rb, d=dst, o2=aop: v.tensor_tensor(
                out=KS(d), in0=mA[:, a::8], in1=mB[:, b::8], op=o2))
        op("gpsimd", lambda g: g.iota(iot[0:64, 0:64], pattern=[[64, 64]], base=0, channel_multiplier=1))
        gm = mark("gpsimd", "gM")
        wait("scalar", "gM", gm)
        op("scalar", lambda s_: s_.activation(out=KS(8), in_=iot[0:64, 0:64], func=AF.Sin, scale=-PI / 4096.0, bias=biasp[0:64, 2:3]))
        op("scalar", lambda s_: s_.activation(out=KS(9), in_=iot[0:64, 0:64], func=AF.Sin, scale=PI / 4096.0))
        sm = mark("scalar", "sM")
        wait("vector", "sM", sm)
        wait("gpsimd", "sM", sm)

        def ar_compute(v):
            co2, s2v = KS(8), KS(9)
            v.tensor_tensor(out=KS(10), in0=co2, in1=KS(6), op=ALU.mult); v.drain()
            v.tensor_scalar(out=KS(10), in0=KS(10), scalar1=1.0, scalar2=None, op0=ALU.add); v.drain()
            v.tensor_tensor(out=KS(11), in0=co2, in1=KS(7), op=ALU.mult); v.drain()
            v.tensor_tensor(out=KS(12), in0=KS(2), in1=KS(4), op=ALU.mult); v.drain()
            v.tensor_tensor(out=KS(14), in0=KS(3), in1=KS(5), op=ALU.mult); v.drain()
            v.tensor_tensor(out=KS(12), in0=KS(12), in1=KS(14), op=ALU.subtract); v.drain()
            v.tensor_tensor(out=KS(13), in0=KS(2), in1=KS(5), op=ALU.mult); v.drain()
            v.tensor_tensor(out=KS(14), in0=KS(3), in1=KS(4), op=ALU.mult); v.drain()
            v.tensor_tensor(out=KS(13), in0=KS(13), in1=KS(14), op=ALU.add); v.drain()
            v.tensor_tensor(out=KS(14), in0=KS(10), in1=KS(10), op=ALU.mult); v.drain()
            v.tensor_tensor(out=KS(15), in0=KS(11), in1=KS(11), op=ALU.mult); v.drain()
            v.tensor_tensor(out=KS(14), in0=KS(14), in1=KS(15), op=ALU.add); v.drain()
            v.reciprocal(out=KS(14), in_=KS(14)); v.drain()
            v.tensor_tensor(out=KS(15), in0=KS(12), in1=KS(10), op=ALU.mult); v.drain()
            v.tensor_tensor(out=t0[0:64, 0:64], in0=KS(13), in1=KS(11), op=ALU.mult); v.drain()
            v.tensor_tensor(out=KS(15), in0=KS(15), in1=t0[0:64, 0:64], op=ALU.add); v.drain()
            v.tensor_tensor(out=t0[0:64, 0:64], in0=KS(13), in1=KS(10), op=ALU.mult); v.drain()
            v.tensor_tensor(out=t0[0:64, 64:128], in0=KS(12), in1=KS(11), op=ALU.mult); v.drain()
            v.tensor_tensor(out=KS(12), in0=t0[0:64, 0:64], in1=t0[0:64, 64:128], op=ALU.subtract); v.drain()
            v.tensor_tensor(out=KS(15), in0=KS(15), in1=KS(14), op=ALU.mult); v.drain()
            v.tensor_tensor(out=KS(12), in0=KS(12), in1=KS(14), op=ALU.mult); v.drain()
            v.tensor_tensor(out=KS(10), in0=co2, in1=KS(15), op=ALU.mult); v.drain()
            v.tensor_tensor(out=KS(10), in0=KS(0), in1=KS(10), op=ALU.subtract); v.drain()
            v.tensor_tensor(out=KS(11), in0=co2, in1=KS(12), op=ALU.mult); v.drain()
            v.tensor_tensor(out=KS(11), in0=KS(1), in1=KS(11), op=ALU.subtract); v.drain()
            v.tensor_tensor(out=KS(13), in0=co2, in1=KS(10), op=ALU.mult); v.drain()
            v.tensor_tensor(out=KS(15), in0=s2v, in1=KS(11), op=ALU.mult); v.drain()
            v.tensor_tensor(out=KS(13), in0=KS(13), in1=KS(15), op=ALU.subtract); v.drain()
            v.tensor_tensor(out=KS(14), in0=co2, in1=KS(11), op=ALU.mult); v.drain()
            v.tensor_tensor(out=KS(15), in0=s2v, in1=KS(10), op=ALU.mult); v.drain()
            v.tensor_tensor(out=KS(14), in0=KS(14), in1=KS(15), op=ALU.add); v.drain()
            v.tensor_copy(out=kbb[:, 64:128], in_=KS(13)); v.drain()
            v.tensor_copy(out=kbb[:, 128:192], in_=KS(14))
        op("vector", ar_compute)
        vm = mark("vector", "vM")
        wait("tensor", "vM", vm)
        opn("tensor", lambda t: t.transpose(ps[6][0:64, 0:32].bitcast(BF16), kbb[:, 64:128], idn[0:64, 0:64]))
        opn("tensor", lambda t: t.transpose(ps[7][0:64, 0:32].bitcast(BF16), kbb[:, 128:192], idn[0:64, 0:64]))
        tm = mark("tensor", "tM")
        wait("vector", "tM", tm)
        op("vector", lambda v: v.tensor_copy(out=kbb[:, 192:256], in_=ps[6][0:64, 0:32].bitcast(BF16)))
        op("vector", lambda v: v.tensor_copy(out=kbb[:, 256:320], in_=ps[7][0:64, 0:32].bitcast(BF16)))
        op("vector", lambda v: v.tensor_scalar(out=t0[0:64, 0:64], in0=ps[7][0:64, 0:32].bitcast(BF16), scalar1=-1.0, scalar2=None, op0=ALU.mult))
        op("vector", lambda v: v.tensor_copy(out=kbb[:, 320:384], in_=t0[0:64, 0:64]))
        vm_ar = mark("vector", "vM")

        op("gpsimd", lambda g: g.iota(iot[0:1, 0:64], pattern=[[1, 64]], base=0, channel_multiplier=0))
        gm = mark("gpsimd", "gM")
        wait("vector", "gM", gm)
        op("vector", lambda v: v.tensor_copy(out=mrow[0:1, 0:64], in_=iot[0:1, 0:64]))
        vm = mark("vector", "vM")
        wait("tensor", "vM", vm)
        opn("tensor", lambda t: t.matmul(ps[6][0:64, 64:128], mrow[0:1, 0:64], mrow[0:1, 0:64], start=True, stop=True))
        tm = mark("tensor", "tM")
        wait("vector", "tM", tm)
        op("vector", lambda v: v.tensor_copy(out=KS(10), in_=ps[6][0:64, 64:128]))

        def dft_red(v):
            prod = KS(10)
            v.tensor_scalar(out=t0[0:64, 0:64], in0=prod, scalar1=1.0 / 64.0, scalar2=1.0 / 128.0, op0=ALU.mult, op1=ALU.add); v.drain()
            v.tensor_copy(out=ti[0:64, 0:64], in_=t0[0:64, 0:64]); v.drain()
            v.tensor_scalar(out=t1[0:64, 0:64], in0=prod, scalar1=1.0 / 64.0, scalar2=None, op0=ALU.mult); v.drain()
            v.tensor_tensor(out=KS(11), in0=t1[0:64, 0:64], in1=ti[0:64, 0:64], op=ALU.subtract); v.drain()
            v.tensor_scalar(out=ti[0:64, 0:64], in0=KS(11), scalar1=0.25, scalar2=None, op0=ALU.add); v.drain()
            v.tensor_tensor(out=KS(11), in0=KS(11), in1=ti[0:64, 0:64], op=ALU.subtract); v.drain()
            v.tensor_scalar(out=t0[0:64, 0:64], in0=prod, scalar1=1.0 / 4096.0, scalar2=1.0 / 8192.0, op0=ALU.mult, op1=ALU.add); v.drain()
            v.tensor_copy(out=ti[0:64, 0:64], in_=t0[0:64, 0:64]); v.drain()
            v.tensor_scalar(out=t1[0:64, 0:64], in0=prod, scalar1=1.0 / 4096.0, scalar2=None, op0=ALU.mult); v.drain()
            v.tensor_tensor(out=KS(12), in0=t1[0:64, 0:64], in1=ti[0:64, 0:64], op=ALU.subtract); v.drain()
            v.tensor_scalar(out=ti[0:64, 0:64], in0=KS(12), scalar1=0.25, scalar2=None, op0=ALU.add); v.drain()
            v.tensor_tensor(out=KS(12), in0=KS(12), in1=ti[0:64, 0:64], op=ALU.subtract)
        op("vector", dft_red)
        vm = mark("vector", "vM")
        wait("scalar", "vM", vm)
        op("scalar", lambda s_: s_.activation(out=KS(13), in_=KS(11), func=AF.Abs))
        op("scalar", lambda s_: s_.activation(out=KS(14), in_=KS(12), func=AF.Abs))
        op("scalar", lambda s_: s_.activation(out=t0[0:64, 0:64], in_=KS(13), func=AF.Sin, scale=-PI, bias=biasp[0:64, 2:3]))
        op("scalar", lambda s_: s_.activation(out=t0[0:64, 64:128], in_=KS(11), func=AF.Sin, scale=PI))
        op("scalar", lambda s_: s_.activation(out=t0[0:64, 128:192], in_=KS(14), func=AF.Sin, scale=-PI, bias=biasp[0:64, 2:3]))
        op("scalar", lambda s_: s_.activation(out=t0[0:64, 192:256], in_=KS(12), func=AF.Sin, scale=PI))
        op("scalar", lambda s_: s_.activation(out=t1[0:64, 0:64], in_=t0[0:64, 64:128], func=AF.Square))
        op("scalar", lambda s_: s_.activation(out=t1[0:64, 64:128], in_=t0[0:64, 192:256], func=AF.Square))
        sm = mark("scalar", "sM")
        wait("vector", "sM", sm)
        op("vector", lambda v: v.tensor_scalar(out=kbb[:, 64 * 6:64 * 7], in0=t1[0:64, 0:64], scalar1=-2.0, scalar2=1.0, op0=ALU.mult, op1=ALU.add))
        op("vector", lambda v: v.scalar_tensor_tensor(out=kbb[:, 64 * 7:64 * 8], in0=t0[0:64, 64:128], scalar=2.0, in1=t0[0:64, 0:64], op0=ALU.mult, op1=ALU.mult))
        op("vector", lambda v: v.tensor_scalar(out=kbb[:, 64 * 8:64 * 9], in0=t1[0:64, 64:128], scalar1=-2.0, scalar2=1.0, op0=ALU.mult, op1=ALU.add))
        op("vector", lambda v: v.scalar_tensor_tensor(out=kbb[:, 64 * 9:64 * 10], in0=t0[0:64, 192:256], scalar=2.0, in1=t0[0:64, 128:192], op0=ALU.mult, op1=ALU.mult))
        vm_dft = mark("vector", "vM")

        W64re = kbb[:, 64 * 6:64 * 7]
        W64im = kbb[:, 64 * 7:64 * 8]
        T4re = kbb[:, 64 * 8:64 * 9]
        T4im = kbb[:, 64 * 9:64 * 10]
        ARre = kbb[:, 192:256]
        ARim = kbb[:, 256:320]
        ARimN = kbb[:, 320:384]
        wait("tensor", "vM", vm_dft)
        opn("tensor", lambda t: t.matmul(ps[6][0:64, 128:192], ARre, W64re, start=True, stop=False))
        opn("tensor", lambda t: t.matmul(ps[6][0:64, 128:192], ARimN, W64im, start=False, stop=True))
        opn("tensor", lambda t: t.matmul(ps[7][0:64, 128:192], ARre, W64im, start=True, stop=False))
        opn("tensor", lambda t: t.matmul(ps[7][0:64, 128:192], ARim, W64re, start=False, stop=True))
        tm = mark("tensor", "tM")
        wait("vector", "tM", tm)

        def twiddle(v):
            Are = t0[0:64, 0:64]; Aim = t0[0:64, 64:128]
            v.tensor_copy(out=Are, in_=ps[6][0:64, 128:192]); v.drain()
            v.tensor_copy(out=Aim, in_=ps[7][0:64, 128:192]); v.drain()
            v.tensor_copy(out=t1[0:64, 0:64], in_=T4re); v.drain()
            v.tensor_copy(out=t1[0:64, 64:128], in_=T4im); v.drain()
            v.tensor_tensor(out=t2[0:64, 0:64], in0=Are, in1=t1[0:64, 0:64], op=ALU.mult); v.drain()
            v.tensor_tensor(out=t2[0:64, 64:128], in0=Aim, in1=t1[0:64, 64:128], op=ALU.mult); v.drain()
            v.tensor_tensor(out=t2[0:64, 0:64], in0=t2[0:64, 0:64], in1=t2[0:64, 64:128], op=ALU.subtract); v.drain()
            v.tensor_tensor(out=t2[0:64, 128:192], in0=Are, in1=t1[0:64, 64:128], op=ALU.mult); v.drain()
            v.tensor_tensor(out=t2[0:64, 192:256], in0=Aim, in1=t1[0:64, 0:64], op=ALU.mult); v.drain()
            v.tensor_tensor(out=t2[0:64, 64:128], in0=t2[0:64, 128:192], in1=t2[0:64, 192:256], op=ALU.add); v.drain()
            v.tensor_copy(out=kbb[:, 192:256], in_=t2[0:64, 0:64]); v.drain()
            v.tensor_copy(out=kbb[:, 256:320], in_=t2[0:64, 64:128]); v.drain()
            v.tensor_scalar(out=t2[0:64, 64:128], in0=t2[0:64, 64:128], scalar1=-1.0, scalar2=None, op0=ALU.mult); v.drain()
            v.tensor_copy(out=kbb[:, 320:384], in_=t2[0:64, 64:128])
        op("vector", twiddle)
        vm = mark("vector", "vM")
        wait("tensor", "vM", vm)
        opn("tensor", lambda t: t.matmul(ps[6][0:64, 192:256], W64re, kbb[:, 192:256], start=True, stop=False))
        opn("tensor", lambda t: t.matmul(ps[6][0:64, 192:256], W64im, kbb[:, 320:384], start=False, stop=True))
        tm = mark("tensor", "tM")
        wait("vector", "tM", tm)
        op("vector", lambda v: v.tensor_scalar(out=t3[0:64, 0:64], in0=ps[6][0:64, 192:256], scalar1=1.0 / 4096.0, scalar2=None, op0=ALU.mult))
        op("vector", lambda v: v.tensor_copy(out=kbb[:, 0:64], in_=t3[0:64, 0:64]))
        op("vector", lambda v: v.memset(z1b[0:1, 0:4096], 0.0))
        vm_k = mark("vector", "vM")
        wait("sync", "vM", vm_k)
        krow_dst = z1b[0:1, 4096:8192].rearrange("o (n2 n1) -> o n2 n1", n1=64)
        dma("sync", "sQ", krow_dst, kbb[:, 0:64])
        wait("sync", "sQ")
        for p in range(128):
            dma("sync", "sQ", kmat[p:p + 1, p:p + 8192], z1b[0:1, 0:8192])
        kmat_done = cnt["sQ"]
        if debug == 1:
            dma("gpsimd", "gQ", dbgK, z1b[0:1, 0:8192])

        def quantize(src_f32, invcol, qbias_col, clip, dst8, W=1024):
            if clip:
                op("vector", lambda v: v.tensor_scalar(out=t4[:, 0:W], in0=src_f32, scalar1=colp[:, invcol:invcol + 1], scalar2=448.0, op0=ALU.mult, op1=ALU.min))
                op("vector", lambda v: v.tensor_scalar(out=t4[:, 0:W], in0=t4[:, 0:W], scalar1=-448.0, scalar2=None, op0=ALU.max))
            else:
                op("vector", lambda v: v.tensor_scalar(out=t4[:, 0:W], in0=src_f32, scalar1=colp[:, invcol:invcol + 1], scalar2=None, op0=ALU.mult))
            vm = mark("vector", "vM")
            wait("scalar", "vM", vm)
            op("scalar", lambda s_: s_.activation(out=t5[:, 0:W], in_=t4[:, 0:W], func=AF.Square))
            op("scalar", lambda s_: s_.activation(out=t5[:, 0:W], in_=t5[:, 0:W], func=AF.Ln, bias=biasp[:, 4:5]))
            sm = mark("scalar", "sM")
            wait("vector", "sM", sm)
            op("vector", lambda v: v.tensor_scalar(out=ti[:, 0:W], in0=t5[:, 0:W], scalar1=0.5 / LN2C, scalar2=-0.5, op0=ALU.mult, op1=ALU.add))
            op("vector", lambda v: v.tensor_scalar(out=ti[:, 0:W], in0=ti[:, 0:W], scalar1=-6, scalar2=8, op0=ALU.max, op1=ALU.min))
            vm = mark("vector", "vM")
            wait("scalar", "vM", vm)
            op("scalar", lambda s_, qb=qbias_col: s_.activation(out=t5[:, 0:W], in_=ti[:, 0:W], func=AF.Exp, scale=LN2C, bias=biasp[:, qb:qb + 1]))
            op("scalar", lambda s_: s_.activation(out=t0[:, 0:W], in_=ti[:, 0:W], func=AF.Exp, scale=-LN2C, bias=biasp[:, 1:2]))
            sm = mark("scalar", "sM")
            wait("vector", "sM", sm)
            op("vector", lambda v: v.tensor_tensor(out=t4[:, 0:W], in0=t4[:, 0:W], in1=t0[:, 0:W], op=ALU.mult))
            op("vector", lambda v: v.tensor_copy(out=ti[:, 0:W], in_=t4[:, 0:W]))
            op("vector", lambda v, d=dst8: v.tensor_tensor(out=d, in0=ti[:, 0:W], in1=t5[:, 0:W], op=ALU.mult))
            return mark("vector", "vM")

        UQ = lambda P, jt: utq[:, 8192 * P + 256 * jt:8192 * P + 256 * (jt + 1)]
        Z1 = lambda P, it: z1b[:, 8192 * P + 256 * it:8192 * P + 256 * (it + 1)]
        wait("sync", "sQ", kmat_done)
        tb_ld = dma("sync", "tQ", toepbig[:, 0:4096], kmat[:, 4096:8192])

        usrc = uT.rearrange("(jt p) s -> p jt s", p=128)
        x1spr = x1sp.rearrange("(t p) s -> p t s", p=128)
        qgr = qg.rearrange("p (t s) -> p t s", s=S)
        qur = qu.rearrange("p (t s) -> p t s", s=S)
        evac_done = {}
        sq_mark = {}
        mm_mark = {}
        conv_end = {}
        params_ps_read = {}

        def emit_stats(qq, it):
            P = qq % 2
            wait("tensor", "vM", evac_done[(qq, it)])
            wait("tensor", "sM", sq_mark[(qq, it)])
            if it == 0 and qq > 0:
                wait("tensor", "vM", params_ps_read[qq - 1])
            par = it % 2
            opn("tensor", lambda t, i=it, P2=P: t.matmul(ps[4][0:1, 0:256], onesc[:, 0:1],
                Z1(P2, i), start=(i == 0), stop=(i == HT - 1)))
            opn("tensor", lambda t, i=it, p2=par: t.matmul(ps[5][0:1, 0:256], onesc[:, 0:1],
                zb8[:, 256 * p2:256 * (p2 + 1)], start=(i == 0), stop=(i == HT - 1)))

        sp_marks = {}
        for qq in range(NQ):
            P = qq % 2
            gq_ut = "gQ" if P == 0 else "gQb"
            if qq >= 2:
                wait("gpsimd", "tM", conv_end[qq - 2])
            for half in range(2):
                dma("gpsimd", gq_ut, utq[:, 8192 * P + 4096 * half:8192 * P + 4096 * (half + 1)].rearrange("p (jt s) -> p jt s", s=256),
                    usrc[:, 16 * half:16 * (half + 1), SQ * qq:SQ * (qq + 1)])
            ut_ld = cnt[gq_ut]
            wait("tensor", gq_ut, ut_ld)
            if qq == 0:
                wait("tensor", "tQ", tb_ld)
                wait("tensor", "vM", cnt["vM"])
            for it in range(HT):
                bank = it % 4
                rp = it % 2
                if qq > 0 or it >= 2:
                    key = (qq, it - 2) if it >= 2 else (qq - 1, HT - 2 + it)
                    wait("sync", "vM", evac_done[key])
                u_ld = dma("sync", "uQ%d" % rp, ursd[:, 256 * rp:256 * (rp + 1)],
                           uT[128 * it:128 * (it + 1), SQ * qq:SQ * (qq + 1)])
                if qq > 0 or it >= 4:
                    key = (qq, it - 4) if it >= 4 else (qq - 1, HT - 4 + it)
                    wait("tensor", "vM", evac_done[key])
                for jt in range(it + 1):
                    opn("tensor", lambda t, b=bank, j=jt, i=it, P2=P: t.matmul(
                        ps[b][:, 0:256], toepbig[:, 128 * (i - j):128 * (i - j) + 128], UQ(P2, j),
                        start=(j == 0), stop=(j == i)))
                if it >= 1:
                    emit_stats(qq, it - 1)
                mm_mark[(qq, it)] = mark("tensor", "tM")
                wait("vector", "tM", mm_mark[(qq, it)])
                wait("vector", "uQ%d" % rp, u_ld)
                op("vector", lambda v, i=it, rp2=rp, b=bank, P2=P: v.scalar_tensor_tensor(
                    out=Z1(P2, i), in0=ursd[:, 256 * rp2:256 * (rp2 + 1)], scalar=colp[:, 0:1],
                    in1=ps[b][:, 0:256], op0=ALU.mult, op1=ALU.add))
                evac_done[(qq, it)] = mark("vector", "vM")
                wait("scalar", "vM", evac_done[(qq, it)])
                op("scalar", lambda s_, i=it, p2=rp, P2=P: s_.activation(out=zb8[:, 256 * p2:256 * (p2 + 1)], in_=Z1(P2, i), func=AF.Square))
                sq_mark[(qq, it)] = mark("scalar", "sM")
            emit_stats(qq, HT - 1)
            conv_end[qq] = mark("tensor", "tM")
            wait("vector", "tM", conv_end[qq])
            op("vector", lambda v: v.tensor_scalar(out=rowf[0:1, 0:256], in0=ps[4][0:1, 0:256], scalar1=1.0 / 4096.0, scalar2=None, op0=ALU.mult))
            op("vector", lambda v: v.tensor_scalar(out=rowf[0:1, 512:768], in0=ps[5][0:1, 0:256], scalar1=1.0 / 4096.0, scalar2=None, op0=ALU.mult))
            params_ps_read[qq] = mark("vector", "vM")
            op("vector", lambda v: v.tensor_tensor(out=rowf[0:1, 1024:1280], in0=rowf[0:1, 0:256], in1=rowf[0:1, 0:256], op=ALU.mult))
            op("vector", lambda v: v.tensor_tensor(out=rowf[0:1, 1024:1280], in0=rowf[0:1, 512:768], in1=rowf[0:1, 1024:1280], op=ALU.subtract))
            op("vector", lambda v: v.tensor_scalar(out=rowf[0:1, 1024:1280], in0=rowf[0:1, 1024:1280], scalar1=1e-5, scalar2=None, op0=ALU.add))
            vm = mark("vector", "vM")
            wait("scalar", "vM", vm)
            op("scalar", lambda s_: s_.activation(out=rowf[0:1, 1280:1536], in_=rowf[0:1, 1024:1280], func=AF.Sqrt))
            sm = mark("scalar", "sM")
            wait("vector", "sM", sm)
            op("vector", lambda v: v.reciprocal(out=rowf[0:1, 1536:1792], in_=rowf[0:1, 1280:1536]))

            def hilo(v):
                v.tensor_copy(out=rowb[0:1, 0:256], in_=rowf[0:1, 0:256]); v.drain()
                v.tensor_copy(out=t4[0:1, 0:256], in_=rowb[0:1, 0:256]); v.drain()
                v.tensor_tensor(out=t4[0:1, 256:512], in0=rowf[0:1, 0:256], in1=t4[0:1, 0:256], op=ALU.subtract); v.drain()
                v.tensor_copy(out=rowb[0:1, 512:768], in_=t4[0:1, 256:512]); v.drain()
                v.tensor_copy(out=rowb[0:1, 1024:1280], in_=rowf[0:1, 1536:1792]); v.drain()
                v.tensor_copy(out=t4[0:1, 0:256], in_=rowb[0:1, 1024:1280]); v.drain()
                v.tensor_tensor(out=t4[0:1, 256:512], in0=rowf[0:1, 1536:1792], in1=t4[0:1, 0:256], op=ALU.subtract); v.drain()
                v.tensor_copy(out=rowb[0:1, 1536:1792], in_=t4[0:1, 256:512])
            op("vector", hilo)
            vm = mark("vector", "vM")
            wait("tensor", "vM", vm)
            opn("tensor", lambda t: t.matmul(ps[6][:, 0:256], ones1[0:1, :], rowb[0:1, 0:256], start=True, stop=False))
            opn("tensor", lambda t: t.matmul(ps[6][:, 0:256], ones1[0:1, :], rowb[0:1, 512:768], start=False, stop=True))
            opn("tensor", lambda t: t.matmul(ps[7][:, 0:256], ones1[0:1, :], rowb[0:1, 1024:1280], start=True, stop=False))
            opn("tensor", lambda t: t.matmul(ps[7][:, 0:256], ones1[0:1, :], rowb[0:1, 1536:1792], start=False, stop=True))
            tm = mark("tensor", "tM")
            wait("vector", "tM", tm)
            for g4 in range(8):
                nb = [t2, t3][g4 % 2]
                if g4 >= 2:
                    wait("vector", "aQ%d" % (g4 % 2), sp_marks[(qq, g4 - 2)])
                elif qq > 0:
                    wait("vector", "aQ%d" % (g4 % 2), sp_marks[(qq - 1, 6 + g4)])
                for j4 in range(4):
                    it = 4 * g4 + j4
                    op("vector", lambda v, i=it, j=j4, nb2=nb, P2=P: v.tensor_tensor(out=nb2[:, 256 * j:256 * (j + 1)], in0=Z1(P2, i), in1=ps[6][:, 0:256], op=ALU.subtract))
                    op("vector", lambda v, j=j4, nb2=nb: v.tensor_tensor(out=nb2[:, 256 * j:256 * (j + 1)], in0=nb2[:, 256 * j:256 * (j + 1)], in1=ps[7][:, 0:256], op=ALU.mult))
                vm = mark("vector", "vM")
                wait("scalar", "vM", vm)
                aqn = "aQ%d" % (g4 % 2)
                dma("scalar", aqn, x1spr[:, 4 * g4:4 * (g4 + 1), SQ * qq:SQ * (qq + 1)],
                    nb[:, 0:1024].rearrange("p (t s) -> p t s", s=256))
                sp_marks[(qq, g4)] = cnt[aqn]
                quantize(nb[:, 0:1024], 1, 0, False, qgr[:, 4 * g4:4 * (g4 + 1), SQ * qq:SQ * (qq + 1)], W=1024)
                if not shared_q:
                    quantize(nb[:, 0:1024], 2, 0, False, qur[:, 4 * g4:4 * (g4 + 1), SQ * qq:SQ * (qq + 1)], W=1024)

        gwr = gwT.rearrange("(ht p) i -> p ht i", p=128)
        upr = upT.rearrange("(ht p) i -> p ht i", p=128)
        GB = lambda par: utq[:, 8192 * par:8192 * par + 4096]
        UB = lambda par: utq[:, 8192 * par + 4096:8192 * (par + 1)]
        mm_done = {}
        bank_free = {}
        spill_done = {}
        ph1_v = cnt["vM"]
        for ic in range(IC):
            par = ic % 2
            if ic == 0:
                wait("gpsimd", "tM", conv_end[2])
            elif ic == 1:
                wait("gpsimd", "tM", conv_end[3])
            if ic >= 2:
                wait("gpsimd", "tM", mm_done[ic - 2])
            gqn = "gQ%d" % par
            for half in range(2):
                dma("gpsimd", gqn, GB(par).rearrange("p (ht i) -> p ht i", i=128)[:, 16 * half:16 * (half + 1)],
                    gwr[:, 16 * half:16 * (half + 1), 128 * ic:128 * (ic + 1)])
                dma("gpsimd", gqn, UB(par).rearrange("p (ht i) -> p ht i", i=128)[:, 16 * half:16 * (half + 1)],
                    upr[:, 16 * half:16 * (half + 1), 128 * ic:128 * (ic + 1)])
            w_ld = cnt[gqn]
            bb = 4 * (ic % 2)
            if ic == 0:
                wait("tensor", "vM", ph1_v)
            if ic >= 2:
                wait("tensor", "vM", bank_free[ic - 2])
            wait("tensor", gqn, w_ld)
            for t32 in range(HT):
                for h2 in range(2):
                    opn("tensor", lambda t, tt2=t32, h=h2, b=bb, p2=par: t.matmul(
                        ps[b + h][:, 0:512], GB(p2)[:, 128 * tt2:128 * (tt2 + 1)],
                        qg[:, S * tt2 + 512 * h:S * tt2 + 512 * (h + 1)],
                        start=(tt2 == 0), stop=(tt2 == HT - 1)))
                    opn("tensor", lambda t, tt2=t32, h=h2, b=bb, p2=par, qx=(qg if shared_q else qu): t.matmul(
                        ps[b + 2 + h][:, 0:512], UB(p2)[:, 128 * tt2:128 * (tt2 + 1)],
                        qx[:, S * tt2 + 512 * h:S * tt2 + 512 * (h + 1)],
                        start=(tt2 == 0), stop=(tt2 == HT - 1)))
            mm_done[ic] = mark("tensor", "tM")
            wait("vector", "tM", mm_done[ic])
            hp = ic % 2
            if ic == 0:
                wait("vector", "aQ0", cnt["aQ0"])
                wait("vector", "aQ1", cnt["aQ1"])
            if ic >= 2:
                wait("vector", "hsQ%d" % hp, spill_done[ic - 2])
            for h2 in range(2):
                op("vector", lambda v, h=h2, b=bb: v.tensor_scalar(out=t0[:, 512 * h:512 * (h + 1)], in0=ps[b + h][:, 0:512], scalar1=colp[:, 4:5], scalar2=None, op0=ALU.mult))
                op("vector", lambda v, h=h2, b=bb: v.tensor_scalar(out=t1[:, 512 * h:512 * (h + 1)], in0=ps[b + 2 + h][:, 0:512], scalar1=colp[:, 5:6], scalar2=None, op0=ALU.mult))
            bank_free[ic] = mark("vector", "vM")
            wait("scalar", "vM", bank_free[ic])
            op("scalar", lambda s_: s_.activation(out=t0[:, 0:512], in_=t0[:, 0:512], func=AF.Silu))
            op("scalar", lambda s_: s_.activation(out=t0[:, 512:1024], in_=t0[:, 512:1024], func=AF.Silu))
            sm = mark("scalar", "sM")
            wait("vector", "sM", sm)
            op("vector", lambda v: v.tensor_tensor(out=t3[:, 0:1024], in0=t0[:, 0:1024], in1=t1[:, 0:1024], op=ALU.mult))
            mark("vector", "vM")
            quantize(t3[:, 0:1024], 3, 3, True, hob[:, 1024 * hp:1024 * (hp + 1)], W=1024)
            wait("sync", "vM", cnt["vM"])
            spill_done[ic] = dma("sync", "hsQ%d" % hp, hidsp[128 * ic:128 * (ic + 1), :], hob[:, 1024 * hp:1024 * (hp + 1)])
            if debug == 1:
                spill_done[ic] = dma("sync", "hsQ%d" % hp, dbghid[128 * ic:128 * (ic + 1), :], hob[:, 1024 * hp:1024 * (hp + 1)])
        ph2a_v = cnt["vM"]

        NP2 = IC // 2
        NHH = 16
        dwr = dwT.rearrange("(b p) m -> p b m", p=128)
        hsr = hidsp.rearrange("(b p) s -> p b s", p=128)
        hq_eng = {0: "sync", 1: "sync", 2: "sync", 3: "sync", 4: "scalar", 5: "scalar", 6: "scalar", 7: "scalar"}
        pair_mark = {}
        evict_mark = {}
        x1bufs = [t0, t1, t4, t5]
        x1_ld = {}
        dq_tgt = {}
        hq_tgt = {}

        def issue_pair(g):
            hh, p = divmod(g, NP2)
            sl = g % NSLOT
            if g >= NSLOT:
                wait("gpsimd", "tM", pair_mark[g - NSLOT])
            else:
                wait("gpsimd", "tM", mm_done[IC - 1])
            dq_tgt[g] = dma("gpsimd", "dQ%d" % sl,
                dbr[:, 512 * sl:512 * (sl + 1)].rearrange("p (b m) -> p b m", m=256),
                dwr[:, 2 * p:2 * p + 2, 256 * hh:256 * (hh + 1)])
            he = hq_eng[sl]
            if g >= NSLOT:
                wait(he, "tM", pair_mark[g - NSLOT])
            else:
                wait(he, "tM", mm_done[IC - 1])
            if hh == 0:
                wait(he, "hsQ0", cnt["hsQ0"])
                wait(he, "hsQ1", cnt["hsQ1"])
            hq_tgt[g] = dma(he, "hQ%d" % sl,
                hbr[:, 2048 * sl:2048 * (sl + 1)].rearrange("p (b s) -> p b s", s=S),
                hsr[:, 2 * p:2 * p + 2, :])

        def issue_x1(it):
            sl = it % 4
            if it < 4:
                wait("scalar", "vM", ph2a_v)
                wait("scalar", "aQ0", cnt["aQ0"])
                wait("scalar", "aQ1", cnt["aQ1"])
            else:
                wait("scalar", "vM", evict_mark[(it - 4) // 2])
            x1_ld[it] = dma("scalar", "xQ%d" % sl, x1bufs[sl][:, 0:1024],
                            x1sp[128 * it:128 * (it + 1), :])

        NPREF = NSLOT
        for g0 in range(NPREF):
            issue_pair(g0)
        for it0 in range(4):
            issue_x1(it0)
        for hh in range(NHH):
            bb = 4 * (hh % 2)
            if hh == 0:
                wait("tensor", "vM", bank_free[IC - 2])
            elif hh == 1:
                wait("tensor", "vM", bank_free[IC - 1])
            else:
                wait("tensor", "vM", evict_mark[hh - 2])
            for p in range(NP2):
                g = hh * NP2 + p
                sl = g % NSLOT
                wait("tensor", "dQ%d" % sl, dq_tgt[g])
                wait("tensor", "hQ%d" % sl, hq_tgt[g])
                for icq in range(2):
                    ic = 2 * p + icq
                    for f2 in range(2):
                        for h2 in range(2):
                            opn("tensor", lambda t, s2=sl, q=icq, f=f2, h=h2, b=bb, i=ic: t.matmul(
                                ps[b + 2 * f + h][:, 0:512],
                                dbr[:, 512 * s2 + 256 * q + 128 * f:512 * s2 + 256 * q + 128 * (f + 1)],
                                hbr[:, 2048 * s2 + 1024 * q + 512 * h:2048 * s2 + 1024 * q + 512 * (h + 1)],
                                start=(i == 0), stop=(i == IC - 1)))
                pair_mark[g] = mark("tensor", "tM")
                if g + NPREF < NHH * NP2:
                    issue_pair(g + NPREF)
            wait("vector", "tM", pair_mark[hh * NP2 + NP2 - 1])
            for f2 in range(2):
                it = 2 * hh + f2
                wait("vector", "xQ%d" % (it % 4), x1_ld[it])
                for h2 in range(2):
                    op("vector", lambda v, f=f2, h=h2, b=bb, i=it: v.scalar_tensor_tensor(
                        out=z2b[:, 1024 * i + 512 * h:1024 * i + 512 * (h + 1)],
                        in0=ps[b + 2 * f + h][:, 0:512], scalar=colp[:, 6:7],
                        in1=x1bufs[i % 4][:, 512 * h:512 * (h + 1)],
                        op0=ALU.mult, op1=ALU.add))
            evict_mark[hh] = mark("vector", "vM")
            if 2 * hh + 4 < HT:
                issue_x1(2 * hh + 4)
            if 2 * hh + 5 < HT:
                issue_x1(2 * hh + 5)

        wait("scalar", "vM", evict_mark[NHH - 1])
        wait("tensor", "vM", evict_mark[NHH - 1])
        sqm = {}
        stat_m = {}
        for it in range(HT):
            sl2 = it % 2
            if it >= 2:
                wait("scalar", "tM", stat_m[it - 2])
            op("scalar", lambda s_, i=it, s2=sl2: s_.activation(
                out=zb8[:, 1024 * s2:1024 * (s2 + 1)],
                in_=z2b[:, 1024 * i:1024 * (i + 1)], func=AF.Square))
            sqm[it] = mark("scalar", "sM")
            wait("tensor", "sM", sqm[it])
            for h2 in range(2):
                opn("tensor", lambda t, h=h2, i=it: t.matmul(
                    ps[h][0:1, 0:512], onesc[:, 0:1],
                    z2b[:, 1024 * i + 512 * h:1024 * i + 512 * (h + 1)],
                    start=(i == 0), stop=(i == HT - 1)))
                opn("tensor", lambda t, h=h2, i=it, s2=sl2: t.matmul(
                    ps[2 + h][0:1, 0:512], onesc[:, 0:1],
                    zb8[:, 1024 * s2 + 512 * h:1024 * s2 + 512 * (h + 1)],
                    start=(i == 0), stop=(i == HT - 1)))
            stat_m[it] = mark("tensor", "tM")
        wait("vector", "tM", cnt["tM"])
        for h2 in range(2):
            op("vector", lambda v, h=h2: v.tensor_scalar(out=rowf[0:1, 512 * h:512 * (h + 1)], in0=ps[h][0:1, 0:512], scalar1=1.0 / 4096.0, scalar2=None, op0=ALU.mult))
            op("vector", lambda v, h=h2: v.tensor_scalar(out=t4[0:1, 0:512], in0=ps[2 + h][0:1, 0:512], scalar1=1.0 / 4096.0, scalar2=None, op0=ALU.mult))
            op("vector", lambda v, h=h2: v.tensor_tensor(out=t4[0:1, 512:1024], in0=rowf[0:1, 512 * h:512 * (h + 1)], in1=rowf[0:1, 512 * h:512 * (h + 1)], op=ALU.mult))
            op("vector", lambda v: v.tensor_tensor(out=t4[0:1, 0:512], in0=t4[0:1, 0:512], in1=t4[0:1, 512:1024], op=ALU.subtract))
            op("vector", lambda v: v.tensor_scalar(out=t4[0:1, 0:512], in0=t4[0:1, 0:512], scalar1=1e-5, scalar2=None, op0=ALU.add))
            vm = mark("vector", "vM")
            wait("scalar", "vM", vm)
            op("scalar", lambda s_: s_.activation(out=t4[0:1, 512:1024], in_=t4[0:1, 0:512], func=AF.Sqrt))
            sm = mark("scalar", "sM")
            wait("vector", "sM", sm)
            op("vector", lambda v, h=h2: v.reciprocal(out=rowf[0:1, 1024 + 512 * h:1024 + 512 * (h + 1)], in_=t4[0:1, 512:1024]))

        def hilo2(v):
            v.tensor_copy(out=rowb[0:1, 0:1024], in_=rowf[0:1, 0:1024]); v.drain()
            v.tensor_copy(out=t4[0:1, 0:1024], in_=rowb[0:1, 0:1024]); v.drain()
            v.tensor_tensor(out=t5[0:1, 0:1024], in0=rowf[0:1, 0:1024], in1=t4[0:1, 0:1024], op=ALU.subtract); v.drain()
            v.tensor_copy(out=rowb[0:1, 1024:2048], in_=t5[0:1, 0:1024]); v.drain()
            v.tensor_copy(out=rowb[0:1, 2048:3072], in_=rowf[0:1, 1024:2048]); v.drain()
            v.tensor_copy(out=t4[0:1, 0:1024], in_=rowb[0:1, 2048:3072]); v.drain()
            v.tensor_tensor(out=t5[0:1, 0:1024], in0=rowf[0:1, 1024:2048], in1=t4[0:1, 0:1024], op=ALU.subtract); v.drain()
            v.tensor_copy(out=rowb[0:1, 3072:4096], in_=t5[0:1, 0:1024])
        op("vector", hilo2)
        vm = mark("vector", "vM")
        wait("tensor", "vM", vm)
        for h2 in range(2):
            opn("tensor", lambda t, h=h2: t.matmul(ps[4 + h][:, 0:512], ones1[0:1, :], rowb[0:1, 512 * h:512 * (h + 1)], start=True, stop=False))
            opn("tensor", lambda t, h=h2: t.matmul(ps[4 + h][:, 0:512], ones1[0:1, :], rowb[0:1, 1024 + 512 * h:1024 + 512 * (h + 1)], start=False, stop=True))
            opn("tensor", lambda t, h=h2: t.matmul(ps[6 + h][:, 0:512], ones1[0:1, :], rowb[0:1, 2048 + 512 * h:2048 + 512 * (h + 1)], start=True, stop=False))
            opn("tensor", lambda t, h=h2: t.matmul(ps[6 + h][:, 0:512], ones1[0:1, :], rowb[0:1, 3072 + 512 * h:3072 + 512 * (h + 1)], start=False, stop=True))
        tm = mark("tensor", "tM")
        wait("vector", "tM", tm)
        o_stm = {}
        for it in range(HT):
            buf = [t3, t2][it % 2]
            oqn = "oQ" if it % 2 == 0 else "oQb"
            if it >= 2:
                wait("vector", oqn, o_stm[it - 2])
            for h2 in range(2):
                op("vector", lambda v, h=h2, i=it, b=buf: v.tensor_tensor(
                    out=b[:, 512 * h:512 * (h + 1)],
                    in0=z2b[:, 1024 * i + 512 * h:1024 * i + 512 * (h + 1)],
                    in1=ps[4 + h][:, 0:512], op=ALU.subtract))
                op("vector", lambda v, h=h2, b=buf: v.tensor_tensor(
                    out=b[:, 512 * h:512 * (h + 1)], in0=b[:, 512 * h:512 * (h + 1)],
                    in1=ps[6 + h][:, 0:512], op=ALU.mult))
            vm = mark("vector", "vM")
            wait("sync", "vM", vm)
            o_stm[it] = dma("sync", oqn, outT[128 * it:128 * (it + 1), :], buf[:, 0:1024])
        wait("tensor", "oQ", cnt["oQ"])
        wait("tensor", "oQb", cnt["oQb"])
        wait("sync", "oQ", cnt["oQ"])
        wait("sync", "oQb", cnt["oQb"])

        @block.sync
        def _(e):
            for f in ops["sync"]:
                f(e)

        @block.scalar
        def _(e):
            for f in ops["scalar"]:
                f(e)

        @block.vector
        def _(e):
            for f in ops["vector"]:
                f(e)

        @block.gpsimd
        def _(e):
            for f in ops["gpsimd"]:
                f(e)

        @block.tensor
        def _(e):
            for f in ops["tensor"]:
                f(e)

    return nc


_NC_CACHE = {}


def kernel(L0_attn_u, L0_attn_Lambda_ri, L0_attn_p_ri, L0_attn_q_ri, L0_attn_Vc_ri,
           L0_attn_Ct_ri, L0_attn_Bp, L0_attn_D, L0_attn_log_step,
           L0_ffn_gate_w, L0_ffn_up_w, L0_ffn_down_w,
           L0_ffn_gate_w_scale, L0_ffn_up_w_scale, L0_ffn_down_w_scale,
           L0_ffn_gate_in_scale, L0_ffn_up_in_scale, L0_ffn_down_in_scale):
    global _NC_CACHE
    u = np.asarray(L0_attn_u, dtype=np.float32).reshape(8192, 4096)
    gwTh = np.ascontiguousarray(np.asarray(L0_ffn_gate_w, dtype=np.float32).T)
    upTh = np.ascontiguousarray(np.asarray(L0_ffn_up_w, dtype=np.float32).T)
    dwTh = np.ascontiguousarray(np.asarray(L0_ffn_down_w, dtype=np.float32).T)
    sca = np.array([[np.float32(np.asarray(L0_attn_D).reshape(-1)[0]),
                     np.float32(np.asarray(L0_attn_log_step).reshape(-1)[0]),
                     np.float32(L0_ffn_gate_w_scale), np.float32(L0_ffn_up_w_scale),
                     np.float32(L0_ffn_down_w_scale), np.float32(L0_ffn_gate_in_scale),
                     np.float32(L0_ffn_up_in_scale), np.float32(L0_ffn_down_in_scale)]],
                   dtype=np.float32)
    common = {
        "gwT": gwTh, "upT": upTh, "dwT": dwTh,
        "lam": np.ascontiguousarray(np.asarray(L0_attn_Lambda_ri, dtype=np.float32)),
        "pri": np.ascontiguousarray(np.asarray(L0_attn_p_ri, dtype=np.float32)),
        "qri": np.ascontiguousarray(np.asarray(L0_attn_q_ri, dtype=np.float32)),
        "vcr": np.ascontiguousarray(np.asarray(L0_attn_Vc_ri, dtype=np.float32)),
        "ctr": np.ascontiguousarray(np.asarray(L0_attn_Ct_ri, dtype=np.float32)),
        "bp": np.ascontiguousarray(np.asarray(L0_attn_Bp, dtype=np.float32)),
        "sca": sca,
    }
    in_maps = []
    for c in range(8):
        m = dict(common)
        m["uT"] = np.ascontiguousarray(u[1024 * c:1024 * (c + 1), :].T)
        in_maps.append(m)
    shared = np.float32(L0_ffn_gate_in_scale) == np.float32(L0_ffn_up_in_scale)
    if shared not in _NC_CACHE:
        _NC_CACHE[shared] = build_program(shared_q=shared)
    res = run_bass_kernel_spmd(_NC_CACHE[shared], in_maps, core_ids=list(range(8)))
    out = np.empty((8192, 4096), dtype=np.float32)
    for c in range(8):
        out[1024 * c:1024 * (c + 1), :] = res.results[c]["outT"].T
    return out.reshape(1, 8192, 4096)


if __name__ == "__main__":
    pass
